# version 11
# baseline (speedup 1.0000x reference)
import numpy as np
from contextlib import ExitStack
import concourse.bass as bass
import concourse.mybir as mybir
from concourse.bass_utils import run_bass_kernel_spmd

F32 = mybir.dt.float32
BF16 = mybir.dt.bfloat16
AF = mybir.ActivationFunctionType
ALU = mybir.AluOpType
AX = mybir.AxisListType

B, S, D = 2, 4096, 4096
H, QR, KVR = 32, 1536, 512
NOPE, ROPE, VD, QD = 128, 64, 128, 192
NEXP, RANK = 28, 8
LORA_SCALING = 2.0
SCALE = float(QD) ** -0.5
LN_EPS = 1e-5
RMS_EPS = 1e-6
NSLOT = 8
CAPS128 = [32, 28, 24, 20, 16, 12, 8, 4]
CAPS512 = [8, 7, 6, 5, 4, 3, 2, 1]
NT = 32
NPAIR = 16

ENGS = ("tensor", "vector", "scalar", "sync", "gpsimd")


def tiles_for(j):
    return [31 - j, 27 - j, 23 - j, 19 - j, 12 + j, 8 + j, 4 + j, j]


NG = 8
NS = 4


class Prog:

    def __init__(self, nc, es):
        self.nc = nc
        self.q = {k: [] for k in ENGS}
        self.sems = {}
        self.cnt = {}
        keys = (["p", "v", "a", "q", "y"]
                + [f"g{i}" for i in range(NG)] + [f"s{i}" for i in range(NS)])
        for name in keys:
            self.sems[name] = es.enter_context(nc.semaphore("sem_" + name))
            self.cnt[name] = 0
        self.ring = {"gpsimd": [f"g{i}" for i in range(NG)],
                     "sync": [f"s{i}" for i in range(NS)]}
        self.ridx = {"gpsimd": 0, "sync": 0}

    def do(self, eng, meth, *args, inc=None, **kw):
        sem = amt = None
        r = None
        if inc is not None:
            key, amt = inc if isinstance(inc, tuple) else (inc, 1)
            self.cnt[key] += amt
            sem = self.sems[key]
            r = self.cnt[key]

        def f(e, meth=meth, args=args, kw=kw, sem=sem, amt=amt):
            ins = getattr(e, meth)(*args, **kw)
            if sem is not None:
                ins.then_inc(sem, amt)

        self.q[eng].append(f)
        return r

    def dma(self, eng, *args, meth="dma_start", **kw):
        key = self.ring[eng][self.ridx[eng]]
        self.ridx[eng] = (self.ridx[eng] + 1) % len(self.ring[eng])
        if self.cnt[key] > 0:
            self.wait(eng, key, self.cnt[key])
        self.do(eng, meth, *args, inc=(key, 16), **kw)
        return (key, self.cnt[key])

    def wait(self, eng, key, val):
        if val <= 0:
            return
        self.q[eng].append(lambda e, s=self.sems[key], v=val: e.wait_ge(s, v))

    def wait_t(self, eng, tok):
        if tok is not None:
            self.wait(eng, tok[0], tok[1])

    def wait_all_dma(self, eng_wait, eng_ring):
        for key in self.ring[eng_ring]:
            self.wait(eng_wait, key, self.cnt[key])

    def mark(self, key):
        return self.cnt[key]

    def barrier(self):
        keys = (["p", "v", "a", "q"]
                + self.ring["gpsimd"] + self.ring["sync"])
        for eng in ENGS:
            for key in keys:
                self.wait(eng, key, self.cnt[key])


def build_nc(debug=0, p0_groups=8, do_p0c=True, p0_stage=100, npairs=NPAIR):
    nc = bass.Bass("TRN2", target_bir_lowering=False)

    dram = {}

    def din(name, shape, dt=F32):
        dram[name] = nc.dram_tensor(name, shape, dt, kind="ExternalInput").ap()
        return dram[name]

    x_full = din("x_full", [S, D])
    x_own = din("x_own", [NSLOT * 128, D])
    wq_a = din("wq_a", [D, QR])
    wq_b = din("wq_b", [NPAIR, QR, 384])
    wkv_a = din("wkv_a", [D, KVR + ROPE])
    wkv_b = din("wkv_b", [NPAIR, KVR, 512])
    wo = din("wo", [D, D])
    base_wt = din("base_wt", [D, D])
    gate_wt = din("gate_wt", [D, NEXP])
    a_t = din("a_t", [D, NEXP * RANK])
    b_t = din("b_t", [NEXP * RANK, D])
    qlnw = din("qlnw", [128, 12])
    kvlnw = din("kvlnw", [128, 4])
    cosq = din("cosq", [128, 1024])
    sinq = din("sinq", [128, 1024])
    cosk = din("cosk", [64, S])
    sink = din("sink", [64, S])
    masks = din("masks", [128, NSLOT * 512])
    rq_in = din("rq", [128, 128])
    rk_in = din("rk", [64, 64])
    ident_in = din("ident", [128, 128])
    esel_in = din("esel", [NEXP, 224])

    out = nc.dram_tensor("out", [NSLOT * 128, D], F32, kind="ExternalOutput").ap()

    o_scr = nc.dram_tensor("o_scr", [H, 128, 1024], BF16, kind="Internal").ap()
    xnT_scr = nc.dram_tensor("xnT_scr", [32, 128, 1024], BF16, kind="Internal").ap()
    kpe_scr = nc.dram_tensor("kpe_scr", [64, S], BF16, kind="Internal").ap()
    outT_scr = nc.dram_tensor("outT_scr", [D, 1024], F32, kind="Internal").ap()

    dbg = {}
    if debug >= 1:
        for name, shape, dt in [
            ("d_cnorm", [128, 4 * S], BF16),
            ("d_kpe", [64, S], BF16),
            ("d_qnorm", [128, 12 * 1024], BF16),
            ("d_xnT", [32, 128, 1024], BF16),
            ("d_misc", [64, 2048], F32),
        ]:
            dbg[name] = nc.dram_tensor(name, shape, dt, kind="ExternalOutput").ap()
    if debug >= 2:
        dbg["d_o"] = nc.dram_tensor("d_o", [H, 128, 1024], BF16, kind="ExternalOutput").ap()
    if debug >= 3:
        dbg["d_hs"] = nc.dram_tensor("d_hs", [224, 1024], BF16, kind="ExternalOutput").ap()
        dbg["d_outT"] = nc.dram_tensor("d_outT", [D, 1024], F32, kind="ExternalOutput").ap()
        dbg["d_lg"] = nc.dram_tensor("d_lg", [128, 224], F32, kind="ExternalOutput").ap()
        dbg["d_gate"] = nc.dram_tensor("d_gate", [128, 224], F32, kind="ExternalOutput").ap()
        dbg["d_gatet"] = nc.dram_tensor("d_gatet", [28, 1024], F32, kind="ExternalOutput").ap()

    es_root = ExitStack()
    with es_root:
        P = Prog(nc, es_root)
        do, wait, mark = P.do, P.wait, P.mark

        def sb(name, shape, dt, stack):
            return stack.enter_context(nc.sbuf_tensor(name, shape, dt))

        def ps(name, stack, shape=None):
            return stack.enter_context(
                nc.psum_tensor(name, shape or [128, 512], F32))

        g_ident = sb("g_ident", [128, 128], F32, es_root)
        g_ones_col = sb("g_ones_col", [128, 1], F32, es_root)
        g_ones_row = sb("g_ones_row", [1, 128], F32, es_root)
        g_qlnw = sb("g_qlnw", [128, 12], F32, es_root)
        g_kvlnw = sb("g_kvlnw", [128, 4], F32, es_root)
        es_mid = ExitStack()
        g_cnorm = [sb(f"g_cnorm{i}", [128, S], BF16, es_mid) for i in range(4)]
        g_kpe = sb("g_kpe", [128, S], BF16, es_mid)
        g_qnorm = sb("g_qnorm", [128, 12 * 1024], BF16, es_mid)

        tok_con = [P.dma("gpsimd", dst[:], src) for dst, src in
                   [(g_ident, ident_in), (g_qlnw, qlnw), (g_kvlnw, kvlnw)]]
        do("gpsimd", "memset", g_ones_col[:], 1.0, inc="q")
        do("gpsimd", "memset", g_ones_row[:], 1.0, inc="q")
        c_consts_q = mark("q")

        es0 = ExitStack()
        with es0:
            x_sb = [sb(f"p0_x{i}", [128, D], BF16, es0) for i in range(2)]
            xn_sb = [sb(f"p0_xn{i}", [128, D], BF16, es0) for i in range(2)]
            xnT_g = sb("p0_xnTg", [128, 32 * 512], BF16, es0)
            wkva_sb = sb("p0_wkva", [128, 32 * 576], BF16, es0)

            rk_sb = sb("p0_rk", [64, 64], BF16, es0)
            c_f32 = sb("p0_cf", [128, 4 * 512], F32, es0)
            sq_f32 = sb("p0_sq", [128, 512], F32, es0)
            kpe_f = sb("p0_kpef", [64, 512], F32, es0)
            kpe_bf = sb("p0_kpebf", [64, 512], BF16, es0)
            kt1 = sb("p0_kt1", [64, 512], F32, es0)
            kt2 = sb("p0_kt2", [64, 512], F32, es0)
            cosf = sb("p0_cosf", [64, 512], F32, es0)
            sinf = sb("p0_sinf", [64, 512], F32, es0)
            bnb = [sb(f"p0_bn{i}", [128, 8 * 6], F32, es0) for i in range(2)]
            mv = [sb(f"p0_mv{i}", [128, 2], F32, es0) for i in range(2)]
            rstd = [sb(f"p0_rstd{i}", [128, 1], F32, es0) for i in range(2)]
            nmr = [sb(f"p0_nmr{i}", [128, 1], F32, es0) for i in range(2)]
            ssr = sb("p0_ssr", [1, 512], F32, es0)
            rsr = sb("p0_rsr", [1, 512], F32, es0)
            rbc = sb("p0_rbc", [128, 512], F32, es0)

            ps_ck = [ps(f"ps_ck{i}", es0) for i in range(3)]
            ps_ss = ps("ps_ss", es0)
            ps_bc = ps("ps_bc", es0)
            ps_rk = ps("ps_rk", es0)

            tok_p0 = [
                P.dma("gpsimd",
                      wkva_sb[:].rearrange("p (k f) -> p k f", k=32),
                      wkv_a.rearrange("(k p) f -> p k f", p=128)),
                P.dma("gpsimd", rk_sb[:], rk_in),
            ]

            ln_state = {}

            def emit_ln(par):
                for i in range(8):
                    do("vector", "bn_stats", bnb[par][:].rearrange("p (n s) -> p n s", s=6)[:, i, :],
                       x_sb[par][:, i * 512:(i + 1) * 512],
                       inc=("v" if i == 7 else None))
                wait("vector", "v", mark("v"))
                do("vector", "bn_aggr", mv[par][:],
                   bnb[par][:].rearrange("p (n s) -> p n s", s=6), inc="v")
                vagg = mark("v")
                wait("vector", "v", vagg)
                do("vector", "tensor_scalar_add", rstd[par][:], mv[par][:, 1:2],
                   LN_EPS, inc="v")
                vagg = mark("v")
                wait("scalar", "v", vagg)
                do("scalar", "activation", rstd[par][:], rstd[par][:], AF.Ln, inc="a")
                a1 = mark("a")
                wait("scalar", "a", a1)
                do("scalar", "activation", rstd[par][:], rstd[par][:], AF.Exp,
                   scale=-0.5, inc="a")
                a2 = mark("a")
                do("vector", "tensor_scalar_mul", nmr[par][:], mv[par][:, 0:1],
                   -1.0, inc="v")
                vn = mark("v")
                wait("vector", "v", vn)
                wait("vector", "a", a2)
                do("vector", "tensor_scalar", xn_sb[par][:], x_sb[par][:],
                   nmr[par][:], rstd[par][:], ALU.add, op1=ALU.mult, inc="v")
                return mark("v")

            tr_marks = [None, None]
            ck_evac = [0, 0, 0]
            xnT_war = 0
            prev_rope_v = 0
            prev_ssr_v = 0
            prev_kpe_a = 0
            prev_prk = 0
            for g in range(p0_groups):
                for tt in range(4):
                    t = g * 4 + tt
                    par = t % 2
                    if t >= 2:
                        wait("gpsimd", "v", ln_state[par])
                    gl = P.dma("gpsimd", x_sb[par][:],
                               x_full[t * 128:(t + 1) * 128, :])
                    P.wait_t("vector", gl)
                    P.wait_t("vector", tr_marks[par])
                    ln_state[par] = emit_ln(par)
                    if p0_stage < 2:
                        continue
                    wait("sync", "v", ln_state[par])
                    if tt == 0:
                        wait("sync", "p", xnT_war)
                    tr_marks[par] = P.dma(
                        "sync",
                        xnT_g[:].rearrange("p (k f) -> p k f", f=512)[:, :, tt * 128:(tt + 1) * 128],
                        xn_sb[par][:].rearrange("p (k f) -> p k f", f=128),
                        meth="dma_start_transpose")
                if p0_stage < 3:
                    continue
                P.wait_t("tensor", tr_marks[0])
                P.wait_t("tensor", tr_marks[1])
                if g == 0:
                    for tk in tok_p0 + tok_con:
                        P.wait_t("tensor", tk)
                    wait("tensor", "q", c_consts_q)
                wait("tensor", "a", prev_kpe_a)
                xnT3 = xnT_g[:].rearrange("p (k f) -> p k f", f=512)
                wk3 = wkva_sb[:].rearrange("p (k f) -> p k f", f=576)
                ssmm = [0, 0]
                first_ss = True
                for mc in range(5 if p0_stage >= 40 else 4):
                    pb = ps_ck[mc % 3]
                    wait("tensor", "v", ck_evac[mc % 3])
                    lo = mc * 128
                    hi = min(lo + 128, 576)
                    for kc in range(32):
                        do("tensor", "matmul",
                           pb[0:hi - lo, :], wk3[:, kc, lo:hi], xnT3[:, kc, :],
                           start=(kc == 0), stop=(kc == 31),
                           inc=("p" if kc == 31 else None))
                    pm = mark("p")
                    if mc < 4:
                        sqs = sq_f32[:]
                        wait("vector", "p", pm)
                        do("vector", "tensor_copy",
                           c_f32[:, mc * 512:(mc + 1) * 512], pb[:], inc="v")
                        cv = mark("v")
                        wait("vector", "v", cv)
                        wait("vector", "p", ssmm[0])
                        do("vector", "tensor_mul", sqs,
                           c_f32[:, mc * 512:(mc + 1) * 512],
                           c_f32[:, mc * 512:(mc + 1) * 512], inc="v")
                        ck_evac[mc % 3] = mark("v")
                        wait("tensor", "v", ck_evac[mc % 3])
                        if first_ss:
                            wait("tensor", "v", prev_ssr_v)
                            first_ss = False
                        do("tensor", "matmul", ps_ss[0:1, :], g_ones_col[:], sqs,
                           start=(mc == 0), stop=(mc == 3), inc="p")
                        ssmm[0] = mark("p")
                    else:
                        wait("vector", "p", pm)
                        do("vector", "tensor_copy", kpe_f[:], pb[0:64, :], inc="v")
                        kv_m = mark("v")
                        wait("scalar", "v", kv_m)
                        wait("scalar", "p", prev_prk)
                        do("scalar", "activation", kpe_bf[:], kpe_f[:], AF.Copy, inc="a")
                        ck_evac[mc % 3] = mark("v")
                kpe_vm, kpe_am = mark("v"), mark("a")
                pss = ssmm[0]
                if p0_stage < 4:
                    continue
                wait("vector", "p", pss)
                do("vector", "tensor_scalar", ssr[:], ps_ss[0:1, :], 1.0 / KVR,
                   RMS_EPS, ALU.mult, op1=ALU.add, inc="v")
                vss = mark("v")
                prev_ssr_v = vss
                wait("scalar", "v", vss)
                do("scalar", "activation", rsr[:], ssr[:], AF.Ln, inc="a")
                am = mark("a")
                wait("scalar", "a", am)
                do("scalar", "activation", rsr[:], rsr[:], AF.Exp, scale=-0.5, inc="a")
                am = mark("a")
                wait("tensor", "a", am)
                do("tensor", "matmul", ps_bc[:], g_ones_row[:], rsr[:],
                   start=True, stop=True, inc="p")
                pbc = mark("p")
                wait("vector", "p", pbc)
                do("vector", "tensor_copy", rbc[:], ps_bc[:], inc="v")
                vbc = mark("v")
                wait("vector", "v", vbc)
                for i in range(4):
                    do("vector", "scalar_tensor_tensor",
                       g_cnorm[i][:, g * 512:(g + 1) * 512],
                       c_f32[:, i * 512:(i + 1) * 512],
                       g_kvlnw[:, i:i + 1], rbc[:], ALU.mult, ALU.mult, inc="v")
                cn_done = mark("v")
                if p0_stage < 42:
                    prev_ssr_v = vss
                    continue
                wait("tensor", "a", kpe_am)
                wait("tensor", "v", prev_rope_v)
                do("tensor", "matmul", ps_rk[0:64, :], rk_sb[:], kpe_bf[:],
                   start=True, stop=True, inc="p")
                prk = mark("p")
                prev_prk = prk
                if p0_stage < 43:
                    continue
                wait("gpsimd", "v", prev_rope_v)
                ck_tok = P.dma("gpsimd", cosf[:], cosk[:, g * 512:(g + 1) * 512])
                sk_tok = P.dma("gpsimd", sinf[:], sink[:, g * 512:(g + 1) * 512])
                P.wait_t("vector", ck_tok)
                P.wait_t("vector", sk_tok)
                wait("vector", "v", kpe_vm)
                do("vector", "tensor_mul", kt1[:], kpe_f[:], cosf[:], inc="v")
                wait("vector", "p", prk)
                do("vector", "tensor_mul", kt2[:], ps_rk[0:64, :], sinf[:], inc="v")
                v6 = mark("v")
                if p0_stage < 44:
                    prev_rope_v = mark("v")
                    continue
                wait("vector", "v", v6)
                do("vector", "tensor_add", g_kpe[0:64, g * 512:(g + 1) * 512],
                   kt1[:], kt2[:], inc="v")
                prev_rope_v = mark("v")
                prev_kpe_a = kpe_am
                if debug >= 1 and g == 7:
                    wait("gpsimd", "v", prev_rope_v)
                    P.dma("gpsimd", dbg["d_misc"][:, 0:512], kpe_f[:])
                    P.dma("gpsimd", dbg["d_misc"][:, 512:1024], kt1[:])
                    P.dma("gpsimd", dbg["d_misc"][:, 1024:1536], kt2[:])
                    P.dma("gpsimd", dbg["d_misc"][:, 1536:2048], cosf[:])
                xnT_war = mark("p")

            p0b_v, p0b_a = mark("v"), mark("a")

        P.barrier()
        if not do_p0c:
            es_mid.close()
            zt = sb("dbg_zero", [128, D], F32, es_root)
            zq = do("gpsimd", "memset", zt[:], 0.0, inc="q")
            wait("gpsimd", "q", zq)
            for t in range(NSLOT):
                P.dma("gpsimd", out[t * 128:(t + 1) * 128, :], zt[:])
            P.wait_all_dma("gpsimd", "gpsimd")
            P.wait_all_dma("gpsimd", "sync")
            block = es_root.enter_context(nc.Block())
            for eng in ENGS:
                getattr(block, eng)(lambda e, q=P.q[eng]: [f(e) for f in q])
            return nc
        es1 = ExitStack()
        with es1:
            x_sb = [sb(f"p1_x{i}", [128, D], BF16, es1) for i in range(2)]
            xn_sb = [sb(f"p1_xn{i}", [128, D], BF16, es1) for i in range(2)]
            xnT_own = sb("p1_xnTown", [128, 32 * 1024], BF16, es1)
            wqa_col = [sb(f"p1_wqa{i}", [128, 32 * 128], BF16, es1) for i in range(2)]
            bnb = [sb(f"p1_bn{i}", [128, 8 * 6], F32, es1) for i in range(2)]
            mv = [sb(f"p1_mv{i}", [128, 2], F32, es1) for i in range(2)]
            rstd = [sb(f"p1_rstd{i}", [128, 1], F32, es1) for i in range(2)]
            nmr = [sb(f"p1_nmr{i}", [128, 1], F32, es1) for i in range(2)]
            qsq = sb("p1_qsq", [128, 512], F32, es1)
            qssr = sb("p1_qssr", [1, 1024], F32, es1)
            qrsr = sb("p1_qrsr", [1, 1024], F32, es1)
            qrbc = sb("p1_qrbc", [128, 1024], F32, es1)
            ps_q = [ps(f"ps_q{i}", es1) for i in range(2)]
            ps_qs = [ps(f"ps_qs{i}", es1) for i in range(2)]
            ps_qb = [ps(f"ps_qb{i}", es1) for i in range(2)]

            tr_marks = [None, None]
            ln2 = {}
            for t in range(8):
                par = t % 2
                if t >= 2:
                    wait("gpsimd", "v", ln2[par])
                gl = P.dma("gpsimd", x_sb[par][:],
                           x_own[t * 128:(t + 1) * 128, :])
                P.wait_t("vector", gl)
                P.wait_t("vector", tr_marks[par])
                for i in range(8):
                    do("vector", "bn_stats",
                       bnb[par][:].rearrange("p (n s) -> p n s", s=6)[:, i, :],
                       x_sb[par][:, i * 512:(i + 1) * 512],
                       inc=("v" if i == 7 else None))
                wait("vector", "v", mark("v"))
                do("vector", "bn_aggr", mv[par][:],
                   bnb[par][:].rearrange("p (n s) -> p n s", s=6), inc="v")
                wait("vector", "v", mark("v"))
                do("vector", "tensor_scalar_add", rstd[par][:], mv[par][:, 1:2],
                   LN_EPS, inc="v")
                wait("scalar", "v", mark("v"))
                do("scalar", "activation", rstd[par][:], rstd[par][:], AF.Ln, inc="a")
                wait("scalar", "a", mark("a"))
                do("scalar", "activation", rstd[par][:], rstd[par][:], AF.Exp,
                   scale=-0.5, inc="a")
                aexp = mark("a")
                do("vector", "tensor_scalar_mul", nmr[par][:], mv[par][:, 0:1],
                   -1.0, inc="v")
                wait("vector", "v", mark("v"))
                wait("vector", "a", aexp)
                do("vector", "tensor_scalar", xn_sb[par][:], x_sb[par][:],
                   nmr[par][:], rstd[par][:], ALU.add, op1=ALU.mult, inc="v")
                ln2[par] = mark("v")
                wait("sync", "v", ln2[par])
                tr_marks[par] = P.dma(
                    "sync",
                    xnT_own[:].rearrange("p (k f) -> p k f", f=1024)[:, :, t * 128:(t + 1) * 128],
                    xn_sb[par][:].rearrange("p (k f) -> p k f", f=128),
                    meth="dma_start_transpose")
            xnT_toks = list(tr_marks)
            for tk in xnT_toks:
                P.wait_t("gpsimd", tk)
            P.dma("gpsimd",
                  xnT_scr.rearrange("k p f -> p k f"),
                  xnT_own[:].rearrange("p (k f) -> p k f", f=1024))
            if debug >= 1:
                P.dma("gpsimd", dbg["d_xnT"].rearrange("k p f -> p k f"),
                      xnT_own[:].rearrange("p (k f) -> p k f", f=1024))
                wait("gpsimd", "v", p0b_v)
                for i in range(4):
                    P.dma("gpsimd", dbg["d_cnorm"][:, i * S:(i + 1) * S],
                          g_cnorm[i][:])
                P.dma("gpsimd", dbg["d_kpe"], g_kpe[0:64, :])

            xo3 = xnT_own[:].rearrange("p (k f) -> p k f", f=1024)
            wq3 = [w[:].rearrange("p (k f) -> p k f", f=128) for w in wqa_col]
            qevac = [0, 0]
            qld_used = [0, 0]
            sq_done = 0
            for mc in range(12):
                par = mc % 2
                if mc >= 2:
                    wait("gpsimd", "p", qld_used[par])
                gl = P.dma("gpsimd",
                           wq3[par],
                           wq_a[:, mc * 128:(mc + 1) * 128].rearrange("(k p) f -> p k f", p=128))
                P.wait_t("tensor", gl)
                if mc == 0:
                    for tk in xnT_toks:
                        P.wait_t("tensor", tk)
                for half in range(2):
                    pb = ps_q[half]
                    wait("tensor", "v", qevac[half])
                    for kc in range(32):
                        do("tensor", "matmul", pb[:],
                           wq3[par][:, kc, :], xo3[:, kc, half * 512:(half + 1) * 512],
                           start=(kc == 0), stop=(kc == 31),
                           inc=("p" if kc == 31 else None))
                    pm = mark("p")
                    qld_used[par] = pm
                    wait("vector", "p", pm)
                    do("vector", "tensor_copy",
                       g_qnorm[:].rearrange("p (k f) -> p k f", f=1024)[:, mc, half * 512:(half + 1) * 512],
                       pb[:], inc="v")
                    vc = mark("v")
                    qnc = g_qnorm[:].rearrange("p (k f) -> p k f", f=1024)[
                        :, mc, half * 512:(half + 1) * 512]
                    wait("vector", "v", vc)
                    do("vector", "tensor_mul", qsq[:], qnc, qnc, inc="v")
                    vs = mark("v")
                    wait("tensor", "v", vs)
                    do("tensor", "matmul", ps_qs[half][0:1, :], g_ones_col[:], qsq[:],
                       start=(mc == 0), stop=(mc == 11), inc="p")
                    pm2 = mark("p")
                    qevac[half] = vs
                    sq_done = pm2
                    wait("vector", "p", sq_done)
            wait("vector", "p", sq_done)
            do("vector", "tensor_scalar", qssr[0:1, 0:512], ps_qs[0][0:1, :],
               1.0 / QR, RMS_EPS, ALU.mult, op1=ALU.add, inc="v")
            do("vector", "tensor_scalar", qssr[0:1, 512:1024], ps_qs[1][0:1, :],
               1.0 / QR, RMS_EPS, ALU.mult, op1=ALU.add, inc="v")
            vq = mark("v")
            wait("scalar", "v", vq)
            do("scalar", "activation", qrsr[:], qssr[:], AF.Ln, inc="a")
            wait("scalar", "a", mark("a"))
            do("scalar", "activation", qrsr[:], qrsr[:], AF.Exp, scale=-0.5, inc="a")
            aq = mark("a")
            wait("tensor", "a", aq)
            do("tensor", "matmul", ps_qb[0][:], g_ones_row[:], qrsr[0:1, 0:512],
               start=True, stop=True)
            do("tensor", "matmul", ps_qb[1][:], g_ones_row[:], qrsr[0:1, 512:1024],
               start=True, stop=True, inc="p")
            pqb = mark("p")
            wait("vector", "p", pqb)
            do("vector", "tensor_copy", qrbc[:, 0:512], ps_qb[0][:])
            do("vector", "tensor_copy", qrbc[:, 512:1024], ps_qb[1][:], inc="v")
            vb = mark("v")
            wait("vector", "v", vb)
            for mc in range(12):
                qn = g_qnorm[:].rearrange("p (k f) -> p k f", f=1024)[:, mc, :]
                do("vector", "scalar_tensor_tensor", qn, qn,
                   g_qlnw[:, mc:mc + 1], qrbc[:], ALU.mult, ALU.mult, inc="v")
            qnorm_done = mark("v")
            if debug >= 1:
                wait("gpsimd", "v", qnorm_done)
                P.dma("gpsimd", dbg["d_qnorm"], g_qnorm[:])

        P.barrier()

        if debug != 1:
            tk1 = P.dma("gpsimd", kpe_scr, g_kpe[0:64, :])
            P.wait_t("gpsimd", tk1)
            tk2 = P.dma("gpsimd", g_kpe[64:128, :], kpe_scr)
            P.wait_t("gpsimd", tk2)

            es2 = ExitStack()
            wqb = sb("p2_wqb", [128, 12 * 384], BF16, es2)
            wkvb = sb("p2_wkvb", [128, 4 * 512], BF16, es2)
            qn_ = [sb(f"p2_qn{i}", [128, 1024], BF16, es2) for i in range(2)]
            qpe = sb("p2_qpe", [128, 1024], BF16, es2)
            xpf = sb("p2_xpf", [128, 1024], F32, es2)
            xpb = sb("p2_xpb", [128, 1024], BF16, es2)
            t1f = sb("p2_t1f", [128, 1024], F32, es2)
            t2f = sb("p2_t2f", [128, 1024], F32, es2)
            kn = [sb(f"p2_kn{i}", [128, S], BF16, es2) for i in range(2)]
            vp = sb("p2_vp", [128, 32 * 256], BF16, es2)
            sco = sb("p2_sco", [128, 4608], F32, es2)
            cmx = [sb(f"p2_cm{i}", [128, 8], F32, es2) for i in range(2)]
            pbuf = [sb(f"p2_pb{i}", [128, 4096], BF16, es2) for i in range(2)]
            ptb = [sb(f"p2_pt{i}", [128, 4096], BF16, es2) for i in range(2)]
            osb = [sb(f"p2_o{i}", [128, 1024], BF16, es2) for i in range(2)]
            nmx = [sb(f"p2_nm{i}", [128, 1], F32, es2) for i in range(2)]
            zs = [sb(f"p2_zs{i}", [128, 1], F32, es2) for i in range(2)]
            zr = [sb(f"p2_zr{i}", [128, 1], F32, es2) for i in range(2)]
            cosq_sb = sb("p2_cosq", [128, 1024], F32, es2)
            sinq_sb = sb("p2_sinq", [128, 1024], F32, es2)
            masks_sb = sb("p2_masks", [128, NSLOT * 512], BF16, es2)
            rq_sb = sb("p2_rq", [128, 128], BF16, es2)
            banks = [ps(f"ps_b{i}", es2) for i in range(8)]

            for dst, src in [(cosq_sb, cosq), (sinq_sb, sinq),
                             (masks_sb, masks), (rq_sb, rq_in)]:
                tk = P.dma("gpsimd", dst[:], src)
                P.wait_t("tensor", tk)
                P.wait_t("vector", tk)
                P.wait_t("scalar", tk)

            wq3 = wqb[:].rearrange("p (k f) -> p k f", f=384)
            wv3 = wkvb[:].rearrange("p (k f) -> p k f", f=512)
            qn3 = g_qnorm[:].rearrange("p (k f) -> p k f", f=1024)
            cn3 = [c[:] for c in g_cnorm]

            bank_free = [("v", 0)] * 8
            pt_tok = [None, None]
            omm_mark = [0, 0]
            exp_mark = [0, 0]
            last_exp_a = 0
            zrec_v = [0, 0]
            o_war = [None, None]
            qmm_mark = kvmm_mark = 0
            wkv_tok_l = [None]
            xpb_war = 0
            rot = [0]
            kn_ready = [0] * npairs
            v_ready = [0] * npairs

            def emit_qkv_A(p):
                nonlocal qmm_mark, xpb_war
                wait("gpsimd", "p", qmm_mark)
                wq_tok = P.dma("gpsimd", wq3,
                               wq_b[p].rearrange("(k p) f -> p k f", p=128))
                wait("gpsimd", "p", kvmm_mark)
                wkv_tok_l[0] = P.dma("gpsimd", wv3,
                                     wkv_b[p].rearrange("(k p) f -> p k f", p=128))
                P.wait_t("tensor", wq_tok)
                xp_v = 0
                for mi, (lo, hi) in enumerate([(0, 128), (128, 256), (256, 384)]):
                    for half in range(2):
                        bi = rot[0] % 6
                        rot[0] += 1
                        wait("tensor", *bank_free[bi])
                        for kc in range(12):
                            do("tensor", "matmul", banks[bi][0:hi - lo, :],
                               wq3[:, kc, lo:hi],
                               qn3[:, kc, half * 512:(half + 1) * 512],
                               start=(kc == 0), stop=(kc == 11),
                               inc=("p" if kc == 11 else None))
                        pmk = mark("p")
                        hsl = slice(half * 512, (half + 1) * 512)
                        if mi < 2:
                            wait("vector", "p", pmk)
                            do("vector", "tensor_scalar_mul", qn_[mi][:, hsl],
                               banks[bi][:], SCALE, inc="v")
                            bank_free[bi] = ("v", mark("v"))
                        else:
                            wait("vector", "p", pmk)
                            do("vector", "tensor_copy", xpf[:, hsl], banks[bi][:],
                               inc="v")
                            xp_v = mark("v")
                            wait("scalar", "v", xp_v)
                            wait("scalar", "p", xpb_war)
                            do("scalar", "activation", xpb[:, hsl], xpf[:, hsl],
                               AF.Copy, inc="a")
                            bank_free[bi] = ("v", xp_v)
                qmm_mark = mark("p")
                xp_a = mark("a")
                for half in range(2):
                    bi = rot[0] % 6
                    rot[0] += 1
                    hsl = slice(half * 512, (half + 1) * 512)
                    wait("tensor", *bank_free[bi])
                    wait("tensor", "a", xp_a)
                    do("tensor", "matmul", banks[bi][:], rq_sb[:], xpb[:, hsl],
                       start=True, stop=True, inc="p")
                    prq = mark("p")
                    wait("vector", "v", xp_v)
                    do("vector", "tensor_mul", t1f[:, hsl], xpf[:, hsl],
                       cosq_sb[:, hsl], inc="v")
                    wait("vector", "p", prq)
                    do("vector", "tensor_mul", t2f[:, hsl], banks[bi][:],
                       sinq_sb[:, hsl], inc="v")
                    v2 = mark("v")
                    wait("vector", "v", v2)
                    do("vector", "tensor_add", qpe[:, hsl], t1f[:, hsl],
                       t2f[:, hsl], inc="v")
                    bank_free[bi] = ("v", v2)
                xpb_war = mark("p")
                P.wait_t("tensor", wkv_tok_l[0])
                for h01 in range(2):
                    for nt in range(8):
                        bi = rot[0] % 6
                        rot[0] += 1
                        wait("tensor", *bank_free[bi])
                        for kc in range(4):
                            do("tensor", "matmul", banks[bi][:],
                               wv3[:, kc, h01 * 128:(h01 + 1) * 128],
                               cn3[kc][:, nt * 512:(nt + 1) * 512],
                               start=(kc == 0), stop=(kc == 3),
                               inc=("p" if kc == 3 else None))
                        pmk = mark("p")
                        wait("vector", "p", pmk)
                        do("vector", "tensor_copy",
                           kn[h01][:, nt * 512:(nt + 1) * 512], banks[bi][:], inc="v")
                        bank_free[bi] = ("v", mark("v"))
                kn_ready[p] = mark("v")

            def emit_qkv_V(p):
                nonlocal kvmm_mark
                vwar = max(omm_mark)
                for tc in range(32):
                    bi = rot[0] % 6
                    rot[0] += 1
                    wait("tensor", *bank_free[bi])
                    for kc in range(4):
                        do("tensor", "matmul", banks[bi][:, 0:256],
                           cn3[kc][:, tc * 128:(tc + 1) * 128],
                           wv3[:, kc, 256:512],
                           start=(kc == 0), stop=(kc == 3),
                           inc=("p" if kc == 3 else None))
                    pmk = mark("p")
                    wait("vector", "p", pmk)
                    if tc == 0:
                        wait("vector", "p", vwar)
                    do("vector", "tensor_copy", vp[:, tc * 256:(tc + 1) * 256],
                       banks[bi][:, 0:256], inc="v")
                    bank_free[bi] = ("v", mark("v"))
                kvmm_mark = mark("p")
                v_ready[p] = mark("v")

            if True:
                def stg1(st):
                    h01, s = st["h01"], st["s"]
                    nk5, so = st["nk5"], st["so"]
                    for c5 in range(nk5):
                        bi = 2 + 2 * h01 + (c5 % 2)
                        sbk = banks[bi]
                        wait("tensor", *bank_free[bi])
                        do("tensor", "matmul", sbk[:],
                           qn_[h01][:, s * 128:(s + 1) * 128],
                           kn[h01][:, c5 * 512:(c5 + 1) * 512],
                           start=True, stop=False)
                        do("tensor", "matmul", sbk[:],
                           qpe[64 * h01:64 * h01 + 64, s * 128:(s + 1) * 128],
                           g_kpe[64 * h01:64 * h01 + 64, c5 * 512:(c5 + 1) * 512],
                           start=False, stop=True, inc="p")
                        pmk = mark("p")
                        csl = slice(so + c5 * 512, so + (c5 + 1) * 512)
                        if c5 == nk5 - 1:
                            wait("vector", "p", pmk)
                            wait("vector", "a", exp_mark[h01])
                            do("vector", "scalar_tensor_tensor", sco[:, csl],
                               sbk[:], 0.0, masks_sb[:, s * 512:(s + 1) * 512],
                               ALU.bypass, ALU.add, inc="v")
                            em = ("v", mark("v"))
                        else:
                            wait("scalar", "p", pmk)
                            wait("scalar", "a", exp_mark[h01])
                            do("scalar", "activation", sco[:, csl], sbk[:],
                               AF.Copy, inc="a")
                            em = ("a", mark("a"))
                        bank_free[bi] = em
                        st.setdefault("ems", []).append(em)
                    h01_, nk5_, so_ = st["h01"], st["nk5"], st["so"]
                    for em in st["ems"]:
                        wait("vector", em[0], em[1])
                    do("vector", "reduce_max", nmx[h01_][:],
                       sco[:, so_:so_ + nk5_ * 512], axis=AX.X, negate=True, inc="v")
                    st["vmx"] = mark("v")

                def stg2(st):
                    h01 = st["h01"]
                    nk5, so = st["nk5"], st["so"]
                    wait("scalar", "v", st["vmx"])
                    wait("scalar", "v", zrec_v[h01])
                    P.wait_t("scalar", pt_tok[h01])
                    do("scalar", "activation", pbuf[h01][:, 0:nk5 * 512],
                       sco[:, so:so + nk5 * 512], AF.Exp, bias=nmx[h01][:],
                       accum_out=zs[h01][:], inc="a")
                    exp_mark[h01] = mark("a")
                    wait("vector", "a", exp_mark[h01])
                    do("vector", "reciprocal", zr[h01][:], zs[h01][:], inc="v")
                    zrec_v[h01] = mark("v")
                    wait("scalar", "v", zrec_v[h01])
                    do("scalar", "mul", pbuf[h01][:, 0:nk5 * 512],
                       pbuf[h01][:, 0:nk5 * 512], zr[h01][:], inc="a")
                    st["pn_a"] = mark("a")

                def stg3(st):
                    h01, s, nk1 = st["h01"], st["s"], st["nk1"]
                    wait("sync", "a", st["pn_a"])
                    wait("sync", "p", omm_mark[h01])
                    pt_tok[h01] = P.dma(
                        "sync",
                        ptb[h01][:].rearrange("p (k f) -> p k f", f=128)[:, 0:nk1, :],
                        pbuf[h01][:].rearrange("p (k f) -> p k f", f=128)[:, 0:nk1, :],
                        meth="dma_start_transpose")
                    pt3 = ptb[h01][:].rearrange("p (k f) -> p k f", f=128)
                    P.wait_t("tensor", pt_tok[h01])
                    wait("tensor", "v", v_ready[st["p"]])
                    wait("tensor", *bank_free[6 + h01])
                    for c1 in range(nk1):
                        do("tensor", "matmul", banks[6 + h01][:, 0:128],
                           vp[:, c1 * 256 + h01 * 128:c1 * 256 + h01 * 128 + 128],
                           pt3[:, c1, :],
                           start=(c1 == 0), stop=(c1 == nk1 - 1),
                           inc=("p" if c1 == nk1 - 1 else None))
                    omm_mark[h01] = mark("p")
                    wait("vector", "p", omm_mark[h01])
                    P.wait_t("vector", o_war[h01])
                    do("vector", "tensor_copy", osb[h01][:, s * 128:(s + 1) * 128],
                       banks[6 + h01][:, 0:128], inc="v")
                    bank_free[6 + h01] = ("v", mark("v"))
                    if (h01 == 0 and s == NSLOT - 1) or (h01 == 1 and s == 0):
                        wait("gpsimd", "v", bank_free[6 + h01][1])
                        o_war[h01] = P.dma("gpsimd", o_scr[2 * st["p"] + h01],
                                           osb[h01][:])

                steps_all = []
                for p in range(npairs):
                    for stp in range(NSLOT):
                        for h01 in range(2):
                            s = stp if h01 == 0 else NSLOT - 1 - stp
                            nk5 = CAPS512[s]
                            steps_all.append(dict(
                                p=p, h01=h01, s=s, nk5=nk5, nk1=CAPS128[s],
                                so=0 if h01 == 0 else (9 - nk5) * 512))
                T = len(steps_all)
                for g in range(T + 2):
                    if g < T and g % 16 == 0:
                        emit_qkv_A(g // 16)
                        if g == 0:
                            emit_qkv_V(0)
                        wait("tensor", "v", kn_ready[g // 16])
                    if g < T:
                        stg1(steps_all[g])
                    if 1 <= g <= T:
                        stg2(steps_all[g - 1])
                    if 2 <= g:
                        stg3(steps_all[g - 2])
                        if (g - 2) % 16 == 15 and (g - 2) // 16 + 1 < npairs:
                            emit_qkv_V((g - 2) // 16 + 1)
            P.barrier()
            es2.close()
            es_mid.close()
            if debug >= 2:
                P.dma("gpsimd", dbg["d_o"], o_scr)

        if debug == 1:
            es_mid.close()
            zt = sb("dbg_zero", [128, D], F32, es_root)
            zq = do("gpsimd", "memset", zt[:], 0.0, inc="q")
            wait("gpsimd", "q", zq)
            for t in range(NSLOT):
                P.dma("gpsimd", out[t * 128:(t + 1) * 128, :], zt[:])
        else:
            es3 = ExitStack()
            hs = [sb(f"p3_hs{i}", [112, 1024], BF16, es3) for i in range(2)]
            es3a = ExitStack()
            at_sb = sb("p3_at", [128, 32 * 224], BF16, es3a)
            gwt_sb = sb("p3_gwt", [128, 32 * 28], BF16, es3a)
            e_sb = sb("p3_esel", [NEXP, 224], F32, es3a)
            xk = [sb(f"p3_xk{i}", [128, 1024], BF16, es3a) for i in range(2)]
            lg_sb = sb("p3_lg", [128, 224], F32, es3a)
            gate_sb = sb("p3_gate", [128, 224], F32, es3a)
            v1s = sb("p3_v1", [128, 1], F32, es3a)
            v2s = sb("p3_v2", [128, 1], F32, es3a)
            is1 = sb("p3_is1", [128, 28], F32, es3a)
            is2 = sb("p3_is2", [128, 28], F32, es3a)
            mskd = sb("p3_mskd", [128, 28], F32, es3a)
            tmpa = sb("p3_tmpa", [128, 28], F32, es3a)
            p1t = sb("p3_p1t", [128, 1], F32, es3a)
            p1o = sb("p3_p1o", [128, 1], F32, es3a)
            p2o = sb("p3_p2o", [128, 1], F32, es3a)
            gate_t = sb("p3_gatet", [NEXP, 1024], F32, es3a)
            gmat = [sb(f"p3_gm{i}", [112, 1024], F32, es3a) for i in range(2)]
            ps_l = ps("ps_l", es3a)
            ps_h = [ps(f"ps_h{i}", es3a) for i in range(4)]
            ps_g = [ps(f"ps_g{i}", es3a) for i in range(2)]

            t_at = P.dma("gpsimd", at_sb[:].rearrange("p (k f) -> p k f", k=32),
                         a_t.rearrange("(k p) f -> p k f", p=128))
            t_gw = P.dma("gpsimd", gwt_sb[:].rearrange("p (k f) -> p k f", k=32),
                         gate_wt.rearrange("(k p) f -> p k f", p=128))
            t_es = P.dma("gpsimd", e_sb[:], esel_in)
            P.wait_t("tensor", t_at)
            P.wait_t("tensor", t_gw)
            at3 = at_sb[:].rearrange("p (k f) -> p k f", f=224)
            gw3 = gwt_sb[:].rearrange("p (k f) -> p k f", f=28)
            xk_used = [0, 0]
            for kc in range(32):
                par = kc % 2
                wait("gpsimd", "p", xk_used[par])
                txk = P.dma("gpsimd", xk[par][:], xnT_scr[kc])
                P.wait_t("tensor", txk)
                for tch in range(8):
                    do("tensor", "matmul", ps_l[:, tch * 28:(tch + 1) * 28],
                       xk[par][:, tch * 128:(tch + 1) * 128], gw3[:, kc, :],
                       start=(kc == 0 and tch == 0), stop=(kc == 31),
                       skip_group_check=True)
                for er in range(2):
                    for half in range(2):
                        do("tensor", "matmul", ps_h[er * 2 + half][0:112, :],
                           at3[:, kc, er * 112:(er + 1) * 112],
                           xk[par][:, half * 512:(half + 1) * 512],
                           start=(kc == 0), stop=(kc == 31),
                           skip_group_check=True,
                           inc=("p" if (er == 1 and half == 1) else None))
                xk_used[par] = mark("p")
            h_done = mark("p")
            wait("vector", "p", h_done)
            do("vector", "tensor_copy", lg_sb[:], ps_l[:, 0:224], inc="v")
            wait("vector", "v", mark("v"))
            gt_marks = []
            for tch in range(8):
                lo = lg_sb[:, tch * 28:(tch + 1) * 28]
                do("vector", "reduce_max", v1s[:], lo, axis=AX.X, inc="v")
                wait("vector", "v", mark("v"))
                do("vector", "tensor_single_scalar", is1[:], lo, v1s[:],
                   ALU.is_ge, inc="v")
                wait("vector", "v", mark("v"))
                do("vector", "scalar_tensor_tensor", mskd[:], is1[:], -1e30,
                   lo, ALU.mult, ALU.add, inc="v")
                wait("vector", "v", mark("v"))
                do("vector", "reduce_max", v2s[:], mskd[:], axis=AX.X, inc="v")
                wait("vector", "v", mark("v"))
                do("vector", "tensor_single_scalar", is2[:], mskd[:], v2s[:],
                   ALU.is_ge, inc="v")
                do("vector", "tensor_sub", p1t[:], v1s[:], v2s[:], inc="v")
                vps = mark("v")
                wait("scalar", "v", vps)
                do("scalar", "activation", p1o[:], p1t[:], AF.Sigmoid, inc="a")
                aps = mark("a")
                wait("vector", "a", aps)
                do("vector", "tensor_scalar", p2o[:], p1o[:], -1.0, 1.0,
                   ALU.mult, op1=ALU.add, inc="v")
                do("vector", "tensor_scalar", tmpa[:], is1[:], p1o[:], None,
                   ALU.mult, inc="v")
                wait("vector", "v", mark("v"))
                do("vector", "tensor_scalar", is2[:], is2[:], p2o[:], None,
                   ALU.mult, inc="v")
                wait("vector", "v", mark("v"))
                do("vector", "tensor_add", gate_sb[:, tch * 28:(tch + 1) * 28],
                   tmpa[:], is2[:], inc="v")
                gv = mark("v")
                wait("tensor", "v", gv)
                do("tensor", "transpose",
                   ps_g[tch // 4][0:NEXP, (tch % 4) * 128:(tch % 4) * 128 + 128],
                   gate_sb[:, tch * 28:(tch + 1) * 28], g_ident[:],
                   inc="p")
                gt_marks.append(mark("p"))
            for gb in range(2):
                wait("vector", "p", gt_marks[gb * 4 + 3])
                do("vector", "tensor_copy", gate_t[:, gb * 512:(gb + 1) * 512],
                   ps_g[gb][0:NEXP, :], inc="v")
            gt_v = mark("v")
            if debug >= 3:
                wait("gpsimd", "v", gt_v)
                P.dma("gpsimd", dbg["d_lg"], lg_sb[:])
                P.dma("gpsimd", dbg["d_gate"], gate_sb[:])
                P.dma("gpsimd", dbg["d_gatet"], gate_t[:])
            P.wait_t("tensor", t_es)
            wait("tensor", "v", gt_v)
            gm_marks = []
            for er in range(2):
                for half in range(2):
                    if er == 1:
                        wait("tensor", "v", gm_marks[half])
                    do("tensor", "matmul", ps_g[half][0:112, :],
                       e_sb[:, er * 112:(er + 1) * 112],
                       gate_t[:, half * 512:(half + 1) * 512],
                       start=True, stop=True, inc="p")
                    gmm = mark("p")
                    wait("vector", "p", gmm)
                    do("vector", "tensor_copy",
                       gmat[er][:, half * 512:(half + 1) * 512],
                       ps_g[half][0:112, :], inc="v")
                    gm_marks.append(mark("v"))
            for er in range(2):
                for half in range(2):
                    wait("vector", "v", gm_marks[er * 2 + half])
                    do("vector", "tensor_mul", hs[er][:, half * 512:(half + 1) * 512],
                       ps_h[er * 2 + half][0:112, :],
                       gmat[er][:, half * 512:(half + 1) * 512], inc="v")
            hs_v = mark("v")
            if debug >= 3:
                wait("gpsimd", "v", hs_v)
                P.dma("gpsimd", dbg["d_hs"][0:112, :], hs[0][:])
                P.dma("gpsimd", dbg["d_hs"][112:224, :], hs[1][:])
            P.barrier()
            es3a.close()

            es3b = ExitStack()
            o_full = sb("p3_of", [128, 32 * 1024], BF16, es3b)
            xn_full = sb("p3_xf", [128, 32 * 1024], BF16, es3b)
            wo_t = [sb(f"p3_wt{i}", [128, 512], BF16, es3b) for i in range(4)]
            bw_t = [sb(f"p3_bw{i}", [128, 512], BF16, es3b) for i in range(4)]
            bt_sb = [sb(f"p3_bt{i}", [112, 4096], BF16, es3b) for i in range(2)]
            xslab = [sb(f"p3_xs{i}", [128, 8 * 512], F32, es3b) for i in range(2)]
            out_st = [sb(f"p3_os{i}", [128, 512], F32, es3b) for i in range(2)]
            ps_acc = [ps(f"ps_acc{i}", es3b) for i in range(8)]

            t_of = P.dma("gpsimd", o_full[:].rearrange("p (k f) -> p k f", k=32),
                         o_scr.rearrange("h p f -> p h f"))
            t_xf = P.dma("gpsimd", xn_full[:].rearrange("p (k f) -> p k f", k=32),
                         xnT_scr.rearrange("h p f -> p h f"))
            tb0 = P.dma("gpsimd", bt_sb[0][:], b_t[0:112, :])
            tb1 = P.dma("gpsimd", bt_sb[1][:], b_t[112:224, :])
            for tk in (t_of, t_xf, tb0, tb1):
                P.wait_t("tensor", tk)
            of3 = o_full[:].rearrange("p (k f) -> p k f", f=1024)
            xf3 = xn_full[:].rearrange("p (k f) -> p k f", f=1024)
            wt_used = [0, 0, 0, 0]
            xs_used = [0, 0]
            acc_free = [("v", 0)] * 8
            os_war = [None, None]
            for dc in range(8):
                pdc = dc % 2
                wait("gpsimd", "v", xs_used[pdc])
                t_xs = P.dma(
                    "gpsimd",
                    xslab[pdc][:].rearrange("p (k f) -> p k f", k=8),
                    x_own[:, dc * 512:(dc + 1) * 512].rearrange("(k p) f -> p k f", p=128))
                P.wait_t("vector", t_xs)
                for kc in range(32):
                    par = kc % 4
                    wait("gpsimd", "p", wt_used[par])
                    t_wo = P.dma("gpsimd", wo_t[par][:],
                                 wo[kc * 128:(kc + 1) * 128, dc * 512:(dc + 1) * 512])
                    t_bw = P.dma("gpsimd", bw_t[par][:],
                                 base_wt[kc * 128:(kc + 1) * 128, dc * 512:(dc + 1) * 512])
                    P.wait_t("tensor", t_wo)
                    P.wait_t("tensor", t_bw)
                    for qc in range(8):
                        if kc == 0:
                            wait("tensor", *acc_free[qc])
                        do("tensor", "matmul", ps_acc[qc][:],
                           of3[:, kc, qc * 128:(qc + 1) * 128], wo_t[par][:],
                           start=(kc == 0), stop=False)
                    for qc in range(8):
                        do("tensor", "matmul", ps_acc[qc][:],
                           xf3[:, kc, qc * 128:(qc + 1) * 128], bw_t[par][:],
                           start=False, stop=False,
                           inc=("p" if qc == 7 else None))
                    wt_used[par] = mark("p")
                qc_done = []
                for qc in range(8):
                    for er in range(2):
                        do("tensor", "matmul", ps_acc[qc][:],
                           hs[er][:, qc * 128:(qc + 1) * 128],
                           bt_sb[er][:, dc * 512:(dc + 1) * 512],
                           start=False, stop=(er == 1),
                           inc=("p" if (er == 1) else None))
                    qc_done.append(mark("p"))
                for qc in range(8):
                    pq = qc % 2
                    wait("vector", "p", qc_done[qc])
                    P.wait_t("vector", os_war[pq])
                    do("vector", "tensor_add", out_st[pq][:], ps_acc[qc][:],
                       xslab[pdc][:, qc * 512:(qc + 1) * 512], inc="v")
                    acc_free[qc] = ("v", mark("v"))
                    wait("gpsimd", "v", acc_free[qc][1])
                    os_war[pq] = P.dma(
                        "gpsimd",
                        out[qc * 128:(qc + 1) * 128, dc * 512:(dc + 1) * 512],
                        out_st[pq][:])
                xs_used[pdc] = mark("v")
            es3b.close()
            es3.close()

        P.wait_all_dma("gpsimd", "gpsimd")
        P.wait_all_dma("gpsimd", "sync")

        block = es_root.enter_context(nc.Block())
        for eng in ENGS:
            getattr(block, eng)(lambda e, q=P.q[eng]: [f(e) for f in q])
    return nc



def _prep_shared(i):
    perm = np.concatenate([np.arange(0, ROPE, 2), np.arange(1, ROPE, 2)])
    wq_b = np.asarray(i["L0_attn_Wq_b"])
    wqb_p = np.empty((NPAIR, QR, 384), np.float32)
    for p in range(NPAIR):
        h0, h1 = 2 * p, 2 * p + 1
        c0 = wq_b[:, h0 * QD:h0 * QD + NOPE]
        c1 = wq_b[:, h1 * QD:h1 * QD + NOPE]
        p0 = wq_b[:, h0 * QD + NOPE:(h0 + 1) * QD][:, perm]
        p1 = wq_b[:, h1 * QD + NOPE:(h1 + 1) * QD][:, perm]
        wqb_p[p] = np.concatenate([c0, c1, p0, p1], axis=1)
    wkv_a = np.array(i["L0_attn_Wkv_a"])
    wkv_a[:, KVR:] = wkv_a[:, KVR:][:, perm]
    wkv_b = np.asarray(i["L0_attn_Wkv_b"])
    wkvb_p = np.empty((NPAIR, KVR, 512), np.float32)
    for p in range(NPAIR):
        h0, h1 = 2 * p, 2 * p + 1
        kn0 = wkv_b[:, h0 * 256:h0 * 256 + NOPE]
        kn1 = wkv_b[:, h1 * 256:h1 * 256 + NOPE]
        v0 = wkv_b[:, h0 * 256 + NOPE:(h0 + 1) * 256]
        v1 = wkv_b[:, h1 * 256 + NOPE:(h1 + 1) * 256]
        wkvb_p[p] = np.concatenate([kn0, kn1, v0, v1], axis=1)
    a_t = np.ascontiguousarray(
        np.asarray(i["L0_ffn_lora_A"]).reshape(NEXP * RANK, D).T)
    b_t = np.ascontiguousarray(
        np.asarray(i["L0_ffn_lora_B"]).transpose(0, 2, 1).reshape(NEXP * RANK, D))
    base_wt = np.ascontiguousarray(np.asarray(i["L0_ffn_base_W"]).T)
    gate_wt = np.ascontiguousarray(np.asarray(i["L0_ffn_gate_W"]).T)
    qlnw = np.ascontiguousarray(
        np.asarray(i["L0_attn_q_a_ln_w"]).reshape(12, 128).T)
    kvlnw = np.ascontiguousarray(
        np.asarray(i["L0_attn_kv_a_ln_w"]).reshape(4, 128).T)

    invf = (1.0 / 10000.0 ** (np.arange(0, ROPE, 2, dtype=np.float64) / ROPE))
    pos = np.arange(S, dtype=np.float64)
    fr = np.outer(invf, pos)
    cosk = np.cos(fr).astype(np.float32)
    sink = np.sin(fr).astype(np.float32)
    cosk = np.concatenate([cosk, cosk], axis=0)
    sink = np.concatenate([sink, sink], axis=0)

    rq = np.zeros((128, 128), np.float32)
    for hb in (0, 64):
        for f in range(32):
            rq[hb + f + 32, hb + f] = -1.0
            rq[hb + f, hb + f + 32] = 1.0
    rk = np.zeros((64, 64), np.float32)
    for f in range(32):
        rk[f + 32, f] = -1.0
        rk[f, f + 32] = 1.0

    ident = np.eye(128, dtype=np.float32)
    esel = np.zeros((NEXP, 224), np.float32)
    for e in range(14):
        esel[e, e * 8:(e + 1) * 8] = LORA_SCALING
    for e in range(14, 28):
        esel[e, 112 + (e - 14) * 8:112 + (e - 13) * 8] = LORA_SCALING

    return dict(
        wq_a=np.ascontiguousarray(np.asarray(i["L0_attn_Wq_a"], np.float32)),
        wq_b=wqb_p, wkv_a=np.ascontiguousarray(wkv_a), wkv_b=wkvb_p,
        wo=np.ascontiguousarray(np.asarray(i["L0_attn_Wo"], np.float32)),
        base_wt=base_wt, gate_wt=gate_wt, a_t=a_t, b_t=b_t,
        qlnw=qlnw, kvlnw=kvlnw, cosk=np.ascontiguousarray(cosk),
        sink=np.ascontiguousarray(sink),
        rq=rq, rk=rk, ident=ident, esel=esel,
    )


def _prep_core(x, shared, j, b):
    tiles = tiles_for(j)
    xb = np.asarray(x[b])
    x_own = np.concatenate([xb[t * 128:(t + 1) * 128] for t in tiles], 0)
    invf = (1.0 / 10000.0 ** (np.arange(0, ROPE, 2, dtype=np.float64) / ROPE))
    posq = np.concatenate(
        [np.arange(t * 128, (t + 1) * 128, dtype=np.float64) for t in tiles])
    fr = np.outer(invf, posq)
    c64 = np.concatenate([np.cos(fr), np.cos(fr)], 0)
    s64 = np.concatenate([np.sin(fr), np.sin(fr)], 0)
    cosq = (np.concatenate([c64, c64], 0) * SCALE).astype(np.float32)
    sinq = (np.concatenate([s64, s64], 0) * SCALE).astype(np.float32)
    masks = np.zeros((128, NSLOT * 512), np.float32)
    for s in range(NSLOT):
        t = tiles[s]
        k0 = CAPS128[s] * 128 - 512
        kk = np.arange(k0, k0 + 512)
        qq = t * 128 + np.arange(128)
        masks[:, s * 512:(s + 1) * 512] = np.where(
            kk[None, :] <= qq[:, None], 0.0, -1e30)
    d = dict(shared)
    d.update(x_full=np.ascontiguousarray(xb), x_own=np.ascontiguousarray(x_own),
             cosq=cosq, sinq=sinq, masks=masks)
    return d


_nc_cache = {}


def _get_nc(debug=0):
    if debug not in _nc_cache:
        _nc_cache[debug] = build_nc(debug)
    return _nc_cache[debug]


def kernel(**inputs):
    shared = _prep_shared(inputs)
    x = np.asarray(inputs["L0_attn_hidden_states"], np.float32)
    in_maps = []
    for c in range(8):
        in_maps.append(_prep_core(x, shared, c % 4, c // 4))
    nc = _get_nc(0)
    res = run_bass_kernel_spmd(nc, in_maps, core_ids=list(range(8)))
    out = np.empty((B, S, D), np.float32)
    for c in range(8):
        tiles = tiles_for(c % 4)
        r = res.results[c]["out"]
        for s, t in enumerate(tiles):
            out[c // 4, t * 128:(t + 1) * 128, :] = r[s * 128:(s + 1) * 128, :]
    return out


# version 12
# speedup vs baseline: 1.1942x; 1.1942x over previous
import numpy as np
from contextlib import ExitStack
import concourse.bass as bass
import concourse.mybir as mybir
from concourse.bass_utils import run_bass_kernel_spmd

F32 = mybir.dt.float32
BF16 = mybir.dt.bfloat16
AF = mybir.ActivationFunctionType
ALU = mybir.AluOpType
AX = mybir.AxisListType

B, S, D = 2, 4096, 4096
H, QR, KVR = 32, 1536, 512
NOPE, ROPE, VD, QD = 128, 64, 128, 192
NEXP, RANK = 28, 8
LORA_SCALING = 2.0
SCALE = float(QD) ** -0.5
LN_EPS = 1e-5
RMS_EPS = 1e-6
NSLOT = 8
CAPS128 = [32, 28, 24, 20, 16, 12, 8, 4]
CAPS512 = [8, 7, 6, 5, 4, 3, 2, 1]
NT = 32
NPAIR = 16

ENGS = ("tensor", "vector", "scalar", "sync", "gpsimd")


def tiles_for(j):
    return [31 - j, 27 - j, 23 - j, 19 - j, 12 + j, 8 + j, 4 + j, j]


NG = 8
NS = 4


class Prog:

    def __init__(self, nc, es):
        self.nc = nc
        self.q = {k: [] for k in ENGS}
        self.sems = {}
        self.cnt = {}
        keys = (["p", "v", "a", "q", "y"]
                + [f"g{i}" for i in range(NG)] + [f"s{i}" for i in range(NS)])
        for name in keys:
            self.sems[name] = es.enter_context(nc.semaphore("sem_" + name))
            self.cnt[name] = 0
        self.ring = {"gpsimd": [f"g{i}" for i in range(NG)],
                     "sync": [f"s{i}" for i in range(NS)]}
        self.ridx = {"gpsimd": 0, "sync": 0}

    def do(self, eng, meth, *args, inc=None, **kw):
        sem = amt = None
        r = None
        if inc is not None:
            key, amt = inc if isinstance(inc, tuple) else (inc, 1)
            self.cnt[key] += amt
            sem = self.sems[key]
            r = self.cnt[key]

        def f(e, meth=meth, args=args, kw=kw, sem=sem, amt=amt):
            ins = getattr(e, meth)(*args, **kw)
            if sem is not None:
                ins.then_inc(sem, amt)

        self.q[eng].append(f)
        return r

    def dma(self, eng, *args, meth="dma_start", **kw):
        key = self.ring[eng][self.ridx[eng]]
        self.ridx[eng] = (self.ridx[eng] + 1) % len(self.ring[eng])
        if self.cnt[key] > 0:
            self.wait(eng, key, self.cnt[key])
        self.do(eng, meth, *args, inc=(key, 16), **kw)
        return (key, self.cnt[key])

    def wait(self, eng, key, val):
        if val <= 0:
            return
        self.q[eng].append(lambda e, s=self.sems[key], v=val: e.wait_ge(s, v))

    def wait_t(self, eng, tok):
        if tok is not None:
            self.wait(eng, tok[0], tok[1])

    def wait_all_dma(self, eng_wait, eng_ring):
        for key in self.ring[eng_ring]:
            self.wait(eng_wait, key, self.cnt[key])

    def mark(self, key):
        return self.cnt[key]

    def barrier(self):
        keys = (["p", "v", "a", "q"]
                + self.ring["gpsimd"] + self.ring["sync"])
        for eng in ENGS:
            for key in keys:
                self.wait(eng, key, self.cnt[key])


def build_nc(debug=0, p0_groups=8, do_p0c=True, p0_stage=100, npairs=NPAIR):
    nc = bass.Bass("TRN2", target_bir_lowering=False)

    dram = {}

    def din(name, shape, dt=F32):
        dram[name] = nc.dram_tensor(name, shape, dt, kind="ExternalInput").ap()
        return dram[name]

    x_full = din("x_full", [S, D])
    x_own = din("x_own", [NSLOT * 128, D])
    wq_a = din("wq_a", [D, QR])
    wq_b = din("wq_b", [NPAIR, QR, 384])
    wkv_a = din("wkv_a", [D, KVR + ROPE])
    wkv_b = din("wkv_b", [NPAIR, KVR, 512])
    wo = din("wo", [D, D])
    base_wt = din("base_wt", [D, D])
    gate_wt = din("gate_wt", [D, NEXP])
    a_t = din("a_t", [D, NEXP * RANK])
    b_t = din("b_t", [NEXP * RANK, D])
    qlnw = din("qlnw", [128, 12])
    kvlnw = din("kvlnw", [128, 4])
    cosq = din("cosq", [128, 1024])
    sinq = din("sinq", [128, 1024])
    cosk = din("cosk", [64, S])
    sink = din("sink", [64, S])
    masks = din("masks", [128, NSLOT * 512])
    rq_in = din("rq", [128, 128])
    rk_in = din("rk", [64, 64])
    ident_in = din("ident", [128, 128])
    esel_in = din("esel", [NEXP, 224])

    out = nc.dram_tensor("out", [NSLOT * 128, D], F32, kind="ExternalOutput").ap()

    o_scr = nc.dram_tensor("o_scr", [H, 128, 1024], BF16, kind="Internal").ap()
    xnT_scr = nc.dram_tensor("xnT_scr", [32, 128, 1024], BF16, kind="Internal").ap()
    kpe_scr = nc.dram_tensor("kpe_scr", [64, S], BF16, kind="Internal").ap()
    outT_scr = nc.dram_tensor("outT_scr", [D, 1024], F32, kind="Internal").ap()

    dbg = {}
    if debug >= 1:
        for name, shape, dt in [
            ("d_cnorm", [128, 4 * S], BF16),
            ("d_kpe", [64, S], BF16),
            ("d_qnorm", [128, 12 * 1024], BF16),
            ("d_xnT", [32, 128, 1024], BF16),
            ("d_misc", [64, 2048], F32),
        ]:
            dbg[name] = nc.dram_tensor(name, shape, dt, kind="ExternalOutput").ap()
    if debug >= 2:
        dbg["d_o"] = nc.dram_tensor("d_o", [H, 128, 1024], BF16, kind="ExternalOutput").ap()
    if debug >= 3:
        dbg["d_hs"] = nc.dram_tensor("d_hs", [224, 1024], BF16, kind="ExternalOutput").ap()
        dbg["d_outT"] = nc.dram_tensor("d_outT", [D, 1024], F32, kind="ExternalOutput").ap()
        dbg["d_lg"] = nc.dram_tensor("d_lg", [128, 224], F32, kind="ExternalOutput").ap()
        dbg["d_gate"] = nc.dram_tensor("d_gate", [128, 224], F32, kind="ExternalOutput").ap()
        dbg["d_gatet"] = nc.dram_tensor("d_gatet", [28, 1024], F32, kind="ExternalOutput").ap()

    es_root = ExitStack()
    with es_root:
        P = Prog(nc, es_root)
        do, wait, mark = P.do, P.wait, P.mark

        def sb(name, shape, dt, stack):
            return stack.enter_context(nc.sbuf_tensor(name, shape, dt))

        def ps(name, stack, shape=None):
            return stack.enter_context(
                nc.psum_tensor(name, shape or [128, 512], F32))

        g_ident = sb("g_ident", [128, 128], F32, es_root)
        g_ones_col = sb("g_ones_col", [128, 1], F32, es_root)
        g_ones_row = sb("g_ones_row", [1, 128], F32, es_root)
        g_qlnw = sb("g_qlnw", [128, 12], F32, es_root)
        g_kvlnw = sb("g_kvlnw", [128, 4], F32, es_root)
        es_mid = ExitStack()
        g_cnorm = [sb(f"g_cnorm{i}", [128, S], BF16, es_mid) for i in range(4)]
        g_kpe = sb("g_kpe", [128, S], BF16, es_mid)
        g_qnorm = sb("g_qnorm", [128, 12 * 1024], BF16, es_mid)

        tok_con = [P.dma("gpsimd", dst[:], src) for dst, src in
                   [(g_ident, ident_in), (g_qlnw, qlnw), (g_kvlnw, kvlnw)]]
        do("gpsimd", "memset", g_ones_col[:], 1.0, inc="q")
        do("gpsimd", "memset", g_ones_row[:], 1.0, inc="q")
        c_consts_q = mark("q")

        es0 = ExitStack()
        with es0:
            x_sb = [sb(f"p0_x{i}", [128, D], BF16, es0) for i in range(2)]
            xn_sb = [sb(f"p0_xn{i}", [128, D], BF16, es0) for i in range(2)]
            xnT_g = sb("p0_xnTg", [128, 32 * 512], BF16, es0)
            wkva_sb = sb("p0_wkva", [128, 32 * 576], BF16, es0)

            rk_sb = sb("p0_rk", [64, 64], BF16, es0)
            c_f32 = sb("p0_cf", [128, 4 * 512], F32, es0)
            sq_f32 = sb("p0_sq", [128, 512], F32, es0)
            kpe_f = sb("p0_kpef", [64, 512], F32, es0)
            kpe_bf = sb("p0_kpebf", [64, 512], BF16, es0)
            kt1 = sb("p0_kt1", [64, 512], F32, es0)
            kt2 = sb("p0_kt2", [64, 512], F32, es0)
            cosf = sb("p0_cosf", [64, 512], F32, es0)
            sinf = sb("p0_sinf", [64, 512], F32, es0)
            bnb = [sb(f"p0_bn{i}", [128, 8 * 6], F32, es0) for i in range(2)]
            mv = [sb(f"p0_mv{i}", [128, 2], F32, es0) for i in range(2)]
            rstd = [sb(f"p0_rstd{i}", [128, 1], F32, es0) for i in range(2)]
            nmr = [sb(f"p0_nmr{i}", [128, 1], F32, es0) for i in range(2)]
            ssr = sb("p0_ssr", [1, 512], F32, es0)
            rsr = sb("p0_rsr", [1, 512], F32, es0)
            rbc = sb("p0_rbc", [128, 512], F32, es0)

            ps_ck = [ps(f"ps_ck{i}", es0) for i in range(3)]
            ps_ss = ps("ps_ss", es0)
            ps_bc = ps("ps_bc", es0)
            ps_rk = ps("ps_rk", es0)

            tok_p0 = [
                P.dma("gpsimd",
                      wkva_sb[:].rearrange("p (k f) -> p k f", k=32),
                      wkv_a.rearrange("(k p) f -> p k f", p=128)),
                P.dma("gpsimd", rk_sb[:], rk_in),
            ]

            ln_state = {}

            def emit_ln(par):
                for i in range(8):
                    do("vector", "bn_stats", bnb[par][:].rearrange("p (n s) -> p n s", s=6)[:, i, :],
                       x_sb[par][:, i * 512:(i + 1) * 512],
                       inc=("v" if i == 7 else None))
                wait("vector", "v", mark("v"))
                do("vector", "bn_aggr", mv[par][:],
                   bnb[par][:].rearrange("p (n s) -> p n s", s=6), inc="v")
                vagg = mark("v")
                wait("vector", "v", vagg)
                do("vector", "tensor_scalar_add", rstd[par][:], mv[par][:, 1:2],
                   LN_EPS, inc="v")
                vagg = mark("v")
                wait("scalar", "v", vagg)
                do("scalar", "activation", rstd[par][:], rstd[par][:], AF.Ln, inc="a")
                a1 = mark("a")
                wait("scalar", "a", a1)
                do("scalar", "activation", rstd[par][:], rstd[par][:], AF.Exp,
                   scale=-0.5, inc="a")
                a2 = mark("a")
                do("vector", "tensor_scalar_mul", nmr[par][:], mv[par][:, 0:1],
                   -1.0, inc="v")
                vn = mark("v")
                wait("vector", "v", vn)
                wait("vector", "a", a2)
                do("vector", "tensor_scalar", xn_sb[par][:], x_sb[par][:],
                   nmr[par][:], rstd[par][:], ALU.add, op1=ALU.mult, inc="v")
                return mark("v")

            tr_marks = [None, None]
            ck_evac = [0, 0, 0]
            xnT_war = 0
            prev_rope_v = 0
            prev_ssr_v = 0
            prev_kpe_a = 0
            prev_prk = 0
            for g in range(p0_groups):
                for tt in range(4):
                    t = g * 4 + tt
                    par = t % 2
                    if t >= 2:
                        wait("gpsimd", "v", ln_state[par])
                    gl = P.dma("gpsimd", x_sb[par][:],
                               x_full[t * 128:(t + 1) * 128, :])
                    P.wait_t("vector", gl)
                    P.wait_t("vector", tr_marks[par])
                    ln_state[par] = emit_ln(par)
                    if p0_stage < 2:
                        continue
                    wait("sync", "v", ln_state[par])
                    if tt == 0:
                        wait("sync", "p", xnT_war)
                    tr_marks[par] = P.dma(
                        "sync",
                        xnT_g[:].rearrange("p (k f) -> p k f", f=512)[:, :, tt * 128:(tt + 1) * 128],
                        xn_sb[par][:].rearrange("p (k f) -> p k f", f=128),
                        meth="dma_start_transpose")
                if p0_stage < 3:
                    continue
                P.wait_t("tensor", tr_marks[0])
                P.wait_t("tensor", tr_marks[1])
                if g == 0:
                    for tk in tok_p0 + tok_con:
                        P.wait_t("tensor", tk)
                    wait("tensor", "q", c_consts_q)
                wait("tensor", "a", prev_kpe_a)
                xnT3 = xnT_g[:].rearrange("p (k f) -> p k f", f=512)
                wk3 = wkva_sb[:].rearrange("p (k f) -> p k f", f=576)
                ssmm = [0, 0]
                first_ss = True
                for mc in range(5 if p0_stage >= 40 else 4):
                    pb = ps_ck[mc % 3]
                    wait("tensor", "v", ck_evac[mc % 3])
                    lo = mc * 128
                    hi = min(lo + 128, 576)
                    for kc in range(32):
                        do("tensor", "matmul",
                           pb[0:hi - lo, :], wk3[:, kc, lo:hi], xnT3[:, kc, :],
                           start=(kc == 0), stop=(kc == 31),
                           inc=("p" if kc == 31 else None))
                    pm = mark("p")
                    if mc < 4:
                        sqs = sq_f32[:]
                        wait("vector", "p", pm)
                        do("vector", "tensor_copy",
                           c_f32[:, mc * 512:(mc + 1) * 512], pb[:], inc="v")
                        cv = mark("v")
                        wait("vector", "v", cv)
                        wait("vector", "p", ssmm[0])
                        do("vector", "tensor_mul", sqs,
                           c_f32[:, mc * 512:(mc + 1) * 512],
                           c_f32[:, mc * 512:(mc + 1) * 512], inc="v")
                        ck_evac[mc % 3] = mark("v")
                        wait("tensor", "v", ck_evac[mc % 3])
                        if first_ss:
                            wait("tensor", "v", prev_ssr_v)
                            first_ss = False
                        do("tensor", "matmul", ps_ss[0:1, :], g_ones_col[:], sqs,
                           start=(mc == 0), stop=(mc == 3), inc="p")
                        ssmm[0] = mark("p")
                    else:
                        wait("vector", "p", pm)
                        do("vector", "tensor_copy", kpe_f[:], pb[0:64, :], inc="v")
                        kv_m = mark("v")
                        wait("scalar", "v", kv_m)
                        wait("scalar", "p", prev_prk)
                        do("scalar", "activation", kpe_bf[:], kpe_f[:], AF.Copy, inc="a")
                        ck_evac[mc % 3] = mark("v")
                kpe_vm, kpe_am = mark("v"), mark("a")
                pss = ssmm[0]
                if p0_stage < 4:
                    continue
                wait("vector", "p", pss)
                do("vector", "tensor_scalar", ssr[:], ps_ss[0:1, :], 1.0 / KVR,
                   RMS_EPS, ALU.mult, op1=ALU.add, inc="v")
                vss = mark("v")
                prev_ssr_v = vss
                wait("scalar", "v", vss)
                do("scalar", "activation", rsr[:], ssr[:], AF.Ln, inc="a")
                am = mark("a")
                wait("scalar", "a", am)
                do("scalar", "activation", rsr[:], rsr[:], AF.Exp, scale=-0.5, inc="a")
                am = mark("a")
                wait("tensor", "a", am)
                do("tensor", "matmul", ps_bc[:], g_ones_row[:], rsr[:],
                   start=True, stop=True, inc="p")
                pbc = mark("p")
                wait("vector", "p", pbc)
                do("vector", "tensor_copy", rbc[:], ps_bc[:], inc="v")
                vbc = mark("v")
                wait("vector", "v", vbc)
                for i in range(4):
                    do("vector", "scalar_tensor_tensor",
                       g_cnorm[i][:, g * 512:(g + 1) * 512],
                       c_f32[:, i * 512:(i + 1) * 512],
                       g_kvlnw[:, i:i + 1], rbc[:], ALU.mult, ALU.mult, inc="v")
                cn_done = mark("v")
                if p0_stage < 42:
                    prev_ssr_v = vss
                    continue
                wait("tensor", "a", kpe_am)
                wait("tensor", "v", prev_rope_v)
                do("tensor", "matmul", ps_rk[0:64, :], rk_sb[:], kpe_bf[:],
                   start=True, stop=True, inc="p")
                prk = mark("p")
                prev_prk = prk
                if p0_stage < 43:
                    continue
                wait("gpsimd", "v", prev_rope_v)
                ck_tok = P.dma("gpsimd", cosf[:], cosk[:, g * 512:(g + 1) * 512])
                sk_tok = P.dma("gpsimd", sinf[:], sink[:, g * 512:(g + 1) * 512])
                P.wait_t("vector", ck_tok)
                P.wait_t("vector", sk_tok)
                wait("vector", "v", kpe_vm)
                do("vector", "tensor_mul", kt1[:], kpe_f[:], cosf[:], inc="v")
                wait("vector", "p", prk)
                do("vector", "tensor_mul", kt2[:], ps_rk[0:64, :], sinf[:], inc="v")
                v6 = mark("v")
                if p0_stage < 44:
                    prev_rope_v = mark("v")
                    continue
                wait("vector", "v", v6)
                do("vector", "tensor_add", g_kpe[0:64, g * 512:(g + 1) * 512],
                   kt1[:], kt2[:], inc="v")
                prev_rope_v = mark("v")
                prev_kpe_a = kpe_am
                if debug >= 1 and g == 7:
                    wait("gpsimd", "v", prev_rope_v)
                    P.dma("gpsimd", dbg["d_misc"][:, 0:512], kpe_f[:])
                    P.dma("gpsimd", dbg["d_misc"][:, 512:1024], kt1[:])
                    P.dma("gpsimd", dbg["d_misc"][:, 1024:1536], kt2[:])
                    P.dma("gpsimd", dbg["d_misc"][:, 1536:2048], cosf[:])
                xnT_war = mark("p")

            p0b_v, p0b_a = mark("v"), mark("a")

        P.barrier()
        if not do_p0c:
            es_mid.close()
            zt = sb("dbg_zero", [128, D], F32, es_root)
            zq = do("gpsimd", "memset", zt[:], 0.0, inc="q")
            wait("gpsimd", "q", zq)
            for t in range(NSLOT):
                P.dma("gpsimd", out[t * 128:(t + 1) * 128, :], zt[:])
            P.wait_all_dma("gpsimd", "gpsimd")
            P.wait_all_dma("gpsimd", "sync")
            block = es_root.enter_context(nc.Block())
            for eng in ENGS:
                getattr(block, eng)(lambda e, q=P.q[eng]: [f(e) for f in q])
            return nc
        es1 = ExitStack()
        with es1:
            x_sb = [sb(f"p1_x{i}", [128, D], BF16, es1) for i in range(2)]
            xn_sb = [sb(f"p1_xn{i}", [128, D], BF16, es1) for i in range(2)]
            xnT_own = sb("p1_xnTown", [128, 32 * 1024], BF16, es1)
            wqa_col = [sb(f"p1_wqa{i}", [128, 32 * 128], BF16, es1) for i in range(2)]
            bnb = [sb(f"p1_bn{i}", [128, 8 * 6], F32, es1) for i in range(2)]
            mv = [sb(f"p1_mv{i}", [128, 2], F32, es1) for i in range(2)]
            rstd = [sb(f"p1_rstd{i}", [128, 1], F32, es1) for i in range(2)]
            nmr = [sb(f"p1_nmr{i}", [128, 1], F32, es1) for i in range(2)]
            qsq = sb("p1_qsq", [128, 512], F32, es1)
            qssr = sb("p1_qssr", [1, 1024], F32, es1)
            qrsr = sb("p1_qrsr", [1, 1024], F32, es1)
            qrbc = sb("p1_qrbc", [128, 1024], F32, es1)
            ps_q = [ps(f"ps_q{i}", es1) for i in range(2)]
            ps_qs = [ps(f"ps_qs{i}", es1) for i in range(2)]
            ps_qb = [ps(f"ps_qb{i}", es1) for i in range(2)]

            tr_marks = [None, None]
            ln2 = {}
            for t in range(8):
                par = t % 2
                if t >= 2:
                    wait("gpsimd", "v", ln2[par])
                gl = P.dma("gpsimd", x_sb[par][:],
                           x_own[t * 128:(t + 1) * 128, :])
                P.wait_t("vector", gl)
                P.wait_t("vector", tr_marks[par])
                for i in range(8):
                    do("vector", "bn_stats",
                       bnb[par][:].rearrange("p (n s) -> p n s", s=6)[:, i, :],
                       x_sb[par][:, i * 512:(i + 1) * 512],
                       inc=("v" if i == 7 else None))
                wait("vector", "v", mark("v"))
                do("vector", "bn_aggr", mv[par][:],
                   bnb[par][:].rearrange("p (n s) -> p n s", s=6), inc="v")
                wait("vector", "v", mark("v"))
                do("vector", "tensor_scalar_add", rstd[par][:], mv[par][:, 1:2],
                   LN_EPS, inc="v")
                wait("scalar", "v", mark("v"))
                do("scalar", "activation", rstd[par][:], rstd[par][:], AF.Ln, inc="a")
                wait("scalar", "a", mark("a"))
                do("scalar", "activation", rstd[par][:], rstd[par][:], AF.Exp,
                   scale=-0.5, inc="a")
                aexp = mark("a")
                do("vector", "tensor_scalar_mul", nmr[par][:], mv[par][:, 0:1],
                   -1.0, inc="v")
                wait("vector", "v", mark("v"))
                wait("vector", "a", aexp)
                do("vector", "tensor_scalar", xn_sb[par][:], x_sb[par][:],
                   nmr[par][:], rstd[par][:], ALU.add, op1=ALU.mult, inc="v")
                ln2[par] = mark("v")
                wait("sync", "v", ln2[par])
                tr_marks[par] = P.dma(
                    "sync",
                    xnT_own[:].rearrange("p (k f) -> p k f", f=1024)[:, :, t * 128:(t + 1) * 128],
                    xn_sb[par][:].rearrange("p (k f) -> p k f", f=128),
                    meth="dma_start_transpose")
            xnT_toks = list(tr_marks)
            for tk in xnT_toks:
                P.wait_t("gpsimd", tk)
            P.dma("gpsimd",
                  xnT_scr.rearrange("k p f -> p k f"),
                  xnT_own[:].rearrange("p (k f) -> p k f", f=1024))
            if debug >= 1:
                P.dma("gpsimd", dbg["d_xnT"].rearrange("k p f -> p k f"),
                      xnT_own[:].rearrange("p (k f) -> p k f", f=1024))
                wait("gpsimd", "v", p0b_v)
                for i in range(4):
                    P.dma("gpsimd", dbg["d_cnorm"][:, i * S:(i + 1) * S],
                          g_cnorm[i][:])
                P.dma("gpsimd", dbg["d_kpe"], g_kpe[0:64, :])

            xo3 = xnT_own[:].rearrange("p (k f) -> p k f", f=1024)
            wq3 = [w[:].rearrange("p (k f) -> p k f", f=128) for w in wqa_col]
            qevac = [0, 0]
            qld_used = [0, 0]
            sq_done = 0
            for mc in range(12):
                par = mc % 2
                if mc >= 2:
                    wait("gpsimd", "p", qld_used[par])
                gl = P.dma("gpsimd",
                           wq3[par],
                           wq_a[:, mc * 128:(mc + 1) * 128].rearrange("(k p) f -> p k f", p=128))
                P.wait_t("tensor", gl)
                if mc == 0:
                    for tk in xnT_toks:
                        P.wait_t("tensor", tk)
                for half in range(2):
                    pb = ps_q[half]
                    wait("tensor", "v", qevac[half])
                    for kc in range(32):
                        do("tensor", "matmul", pb[:],
                           wq3[par][:, kc, :], xo3[:, kc, half * 512:(half + 1) * 512],
                           start=(kc == 0), stop=(kc == 31),
                           inc=("p" if kc == 31 else None))
                    pm = mark("p")
                    qld_used[par] = pm
                    wait("vector", "p", pm)
                    do("vector", "tensor_copy",
                       g_qnorm[:].rearrange("p (k f) -> p k f", f=1024)[:, mc, half * 512:(half + 1) * 512],
                       pb[:], inc="v")
                    vc = mark("v")
                    qnc = g_qnorm[:].rearrange("p (k f) -> p k f", f=1024)[
                        :, mc, half * 512:(half + 1) * 512]
                    wait("vector", "v", vc)
                    do("vector", "tensor_mul", qsq[:], qnc, qnc, inc="v")
                    vs = mark("v")
                    wait("tensor", "v", vs)
                    do("tensor", "matmul", ps_qs[half][0:1, :], g_ones_col[:], qsq[:],
                       start=(mc == 0), stop=(mc == 11), inc="p")
                    pm2 = mark("p")
                    qevac[half] = vs
                    sq_done = pm2
                    wait("vector", "p", sq_done)
            wait("vector", "p", sq_done)
            do("vector", "tensor_scalar", qssr[0:1, 0:512], ps_qs[0][0:1, :],
               1.0 / QR, RMS_EPS, ALU.mult, op1=ALU.add, inc="v")
            do("vector", "tensor_scalar", qssr[0:1, 512:1024], ps_qs[1][0:1, :],
               1.0 / QR, RMS_EPS, ALU.mult, op1=ALU.add, inc="v")
            vq = mark("v")
            wait("scalar", "v", vq)
            do("scalar", "activation", qrsr[:], qssr[:], AF.Ln, inc="a")
            wait("scalar", "a", mark("a"))
            do("scalar", "activation", qrsr[:], qrsr[:], AF.Exp, scale=-0.5, inc="a")
            aq = mark("a")
            wait("tensor", "a", aq)
            do("tensor", "matmul", ps_qb[0][:], g_ones_row[:], qrsr[0:1, 0:512],
               start=True, stop=True)
            do("tensor", "matmul", ps_qb[1][:], g_ones_row[:], qrsr[0:1, 512:1024],
               start=True, stop=True, inc="p")
            pqb = mark("p")
            wait("vector", "p", pqb)
            do("vector", "tensor_copy", qrbc[:, 0:512], ps_qb[0][:])
            do("vector", "tensor_copy", qrbc[:, 512:1024], ps_qb[1][:], inc="v")
            vb = mark("v")
            wait("vector", "v", vb)
            for mc in range(12):
                qn = g_qnorm[:].rearrange("p (k f) -> p k f", f=1024)[:, mc, :]
                do("vector", "scalar_tensor_tensor", qn, qn,
                   g_qlnw[:, mc:mc + 1], qrbc[:], ALU.mult, ALU.mult, inc="v")
            qnorm_done = mark("v")
            if debug >= 1:
                wait("gpsimd", "v", qnorm_done)
                P.dma("gpsimd", dbg["d_qnorm"], g_qnorm[:])

        P.barrier()

        if debug != 1:
            tk1 = P.dma("gpsimd", kpe_scr, g_kpe[0:64, :])
            P.wait_t("gpsimd", tk1)
            tk2 = P.dma("gpsimd", g_kpe[64:128, :], kpe_scr)
            P.wait_t("gpsimd", tk2)

            es2 = ExitStack()
            wqb = sb("p2_wqb", [128, 12 * 384], BF16, es2)
            wkvb = sb("p2_wkvb", [128, 4 * 512], BF16, es2)
            qn_ = [sb(f"p2_qn{i}", [128, 1024], BF16, es2) for i in range(2)]
            qpe = sb("p2_qpe", [128, 1024], BF16, es2)
            xpf = sb("p2_xpf", [128, 1024], F32, es2)
            xpb = sb("p2_xpb", [128, 1024], BF16, es2)
            t1f = sb("p2_t1f", [128, 1024], F32, es2)
            t2f = sb("p2_t2f", [128, 1024], F32, es2)
            kn = [sb(f"p2_kn{i}", [128, S], BF16, es2) for i in range(2)]
            vp = sb("p2_vp", [128, 32 * 256], BF16, es2)
            sco = sb("p2_sco", [128, 4608], F32, es2)
            cmx = [sb(f"p2_cm{i}", [128, 8], F32, es2) for i in range(2)]
            pbuf = [sb(f"p2_pb{i}", [128, 4096], BF16, es2) for i in range(2)]
            ptb = [sb(f"p2_pt{i}", [128, 4096], BF16, es2) for i in range(2)]
            osb = [sb(f"p2_o{i}", [128, 1024], BF16, es2) for i in range(2)]
            nmx = [sb(f"p2_nm{i}", [128, 1], F32, es2) for i in range(2)]
            zs = [sb(f"p2_zs{i}", [128, 1], F32, es2) for i in range(2)]
            zr = [sb(f"p2_zr{i}", [128, 1], F32, es2) for i in range(2)]
            cosq_sb = sb("p2_cosq", [128, 1024], F32, es2)
            sinq_sb = sb("p2_sinq", [128, 1024], F32, es2)
            masks_sb = sb("p2_masks", [128, NSLOT * 512], BF16, es2)
            rq_sb = sb("p2_rq", [128, 128], BF16, es2)
            banks = [ps(f"ps_b{i}", es2) for i in range(8)]

            for dst, src in [(cosq_sb, cosq), (sinq_sb, sinq),
                             (masks_sb, masks), (rq_sb, rq_in)]:
                tk = P.dma("gpsimd", dst[:], src)
                P.wait_t("tensor", tk)
                P.wait_t("vector", tk)
                P.wait_t("scalar", tk)

            wq3 = wqb[:].rearrange("p (k f) -> p k f", f=384)
            wv3 = wkvb[:].rearrange("p (k f) -> p k f", f=512)
            qn3 = g_qnorm[:].rearrange("p (k f) -> p k f", f=1024)
            cn3 = [c[:] for c in g_cnorm]

            bank_free = [("v", 0)] * 8
            pt_tok = [None, None]
            omm_mark = [0, 0]
            exp_mark = [0, 0]
            last_exp_a = 0
            zrec_v = [0, 0]
            o_war = [None, None]
            qmm_mark = kvmm_mark = 0
            wkv_tok_l = [None]
            xpb_war = 0
            rot = [0]
            kn_ready = [0] * npairs
            v_ready = [0] * npairs

            def emit_qkv_A(p):
                nonlocal qmm_mark, xpb_war
                wait("gpsimd", "p", qmm_mark)
                wq_tok = P.dma("gpsimd", wq3,
                               wq_b[p].rearrange("(k p) f -> p k f", p=128))
                wait("gpsimd", "p", kvmm_mark)
                wkv_tok_l[0] = P.dma("gpsimd", wv3,
                                     wkv_b[p].rearrange("(k p) f -> p k f", p=128))
                P.wait_t("tensor", wq_tok)
                xp_v = 0
                for mi, (lo, hi) in enumerate([(0, 128), (128, 256), (256, 384)]):
                    for half in range(2):
                        bi = rot[0] % 6
                        rot[0] += 1
                        wait("tensor", *bank_free[bi])
                        for kc in range(12):
                            do("tensor", "matmul", banks[bi][0:hi - lo, :],
                               wq3[:, kc, lo:hi],
                               qn3[:, kc, half * 512:(half + 1) * 512],
                               start=(kc == 0), stop=(kc == 11),
                               inc=("p" if kc == 11 else None))
                        pmk = mark("p")
                        hsl = slice(half * 512, (half + 1) * 512)
                        if mi < 2:
                            wait("vector", "p", pmk)
                            do("vector", "tensor_scalar_mul", qn_[mi][:, hsl],
                               banks[bi][:], SCALE, inc="v")
                            bank_free[bi] = ("v", mark("v"))
                        else:
                            wait("vector", "p", pmk)
                            do("vector", "tensor_copy", xpf[:, hsl], banks[bi][:],
                               inc="v")
                            xp_v = mark("v")
                            wait("scalar", "v", xp_v)
                            wait("scalar", "p", xpb_war)
                            do("scalar", "activation", xpb[:, hsl], xpf[:, hsl],
                               AF.Copy, inc="a")
                            bank_free[bi] = ("v", xp_v)
                qmm_mark = mark("p")
                xp_a = mark("a")
                for half in range(2):
                    bi = rot[0] % 6
                    rot[0] += 1
                    hsl = slice(half * 512, (half + 1) * 512)
                    wait("tensor", *bank_free[bi])
                    wait("tensor", "a", xp_a)
                    do("tensor", "matmul", banks[bi][:], rq_sb[:], xpb[:, hsl],
                       start=True, stop=True, inc="p")
                    prq = mark("p")
                    wait("vector", "v", xp_v)
                    do("vector", "tensor_mul", t1f[:, hsl], xpf[:, hsl],
                       cosq_sb[:, hsl], inc="v")
                    wait("vector", "p", prq)
                    do("vector", "tensor_mul", t2f[:, hsl], banks[bi][:],
                       sinq_sb[:, hsl], inc="v")
                    v2 = mark("v")
                    wait("vector", "v", v2)
                    do("vector", "tensor_add", qpe[:, hsl], t1f[:, hsl],
                       t2f[:, hsl], inc="v")
                    bank_free[bi] = ("v", v2)
                xpb_war = mark("p")
                P.wait_t("tensor", wkv_tok_l[0])
                for h01 in range(2):
                    for nt in range(8):
                        bi = rot[0] % 6
                        rot[0] += 1
                        wait("tensor", *bank_free[bi])
                        for kc in range(4):
                            do("tensor", "matmul", banks[bi][:],
                               wv3[:, kc, h01 * 128:(h01 + 1) * 128],
                               cn3[kc][:, nt * 512:(nt + 1) * 512],
                               start=(kc == 0), stop=(kc == 3),
                               inc=("p" if kc == 3 else None))
                        pmk = mark("p")
                        wait("vector", "p", pmk)
                        do("vector", "tensor_copy",
                           kn[h01][:, nt * 512:(nt + 1) * 512], banks[bi][:], inc="v")
                        bank_free[bi] = ("v", mark("v"))
                kn_ready[p] = mark("v")

            def emit_qkv_V(p):
                nonlocal kvmm_mark
                vwar = max(omm_mark)
                for tc in range(32):
                    bi = rot[0] % 6
                    rot[0] += 1
                    wait("tensor", *bank_free[bi])
                    for kc in range(4):
                        do("tensor", "matmul", banks[bi][:, 0:256],
                           cn3[kc][:, tc * 128:(tc + 1) * 128],
                           wv3[:, kc, 256:512],
                           start=(kc == 0), stop=(kc == 3),
                           inc=("p" if kc == 3 else None))
                    pmk = mark("p")
                    wait("vector", "p", pmk)
                    if tc == 0:
                        wait("vector", "p", vwar)
                    do("vector", "tensor_copy", vp[:, tc * 256:(tc + 1) * 256],
                       banks[bi][:, 0:256], inc="v")
                    bank_free[bi] = ("v", mark("v"))
                kvmm_mark = mark("p")
                v_ready[p] = mark("v")

            if True:
                def stg1(st):
                    h01, s = st["h01"], st["s"]
                    nk5, so = st["nk5"], st["so"]
                    for c5 in range(nk5):
                        bi = 2 + 2 * h01 + (c5 % 2)
                        sbk = banks[bi]
                        wait("tensor", *bank_free[bi])
                        do("tensor", "matmul", sbk[:],
                           qn_[h01][:, s * 128:(s + 1) * 128],
                           kn[h01][:, c5 * 512:(c5 + 1) * 512],
                           start=True, stop=False)
                        do("tensor", "matmul", sbk[:],
                           qpe[64 * h01:64 * h01 + 64, s * 128:(s + 1) * 128],
                           g_kpe[64 * h01:64 * h01 + 64, c5 * 512:(c5 + 1) * 512],
                           start=False, stop=True, inc="p")
                        pmk = mark("p")
                        csl = slice(so + c5 * 512, so + (c5 + 1) * 512)
                        if c5 == nk5 - 1:
                            wait("vector", "p", pmk)
                            wait("vector", "a", exp_mark[h01])
                            do("vector", "scalar_tensor_tensor", sco[:, csl],
                               sbk[:], 0.0, masks_sb[:, s * 512:(s + 1) * 512],
                               ALU.bypass, ALU.add, inc="v")
                            em = ("v", mark("v"))
                        else:
                            wait("scalar", "p", pmk)
                            wait("scalar", "a", exp_mark[h01])
                            do("scalar", "activation", sco[:, csl], sbk[:],
                               AF.Copy, inc="a")
                            em = ("a", mark("a"))
                        bank_free[bi] = em
                        st.setdefault("ems", []).append(em)
                    h01_, nk5_, so_ = st["h01"], st["nk5"], st["so"]
                    for em in st["ems"]:
                        wait("vector", em[0], em[1])
                    wait("vector", "a", exp_mark[h01_])
                    do("vector", "reduce_max", nmx[h01_][:],
                       sco[:, so_:so_ + nk5_ * 512], axis=AX.X, negate=True, inc="v")
                    st["vmx"] = mark("v")

                def stg2(st):
                    h01 = st["h01"]
                    nk5, so = st["nk5"], st["so"]
                    wait("scalar", "v", st["vmx"])
                    wait("scalar", "v", zrec_v[h01])
                    P.wait_t("scalar", pt_tok[h01])
                    do("scalar", "activation", pbuf[h01][:, 0:nk5 * 512],
                       sco[:, so:so + nk5 * 512], AF.Exp, bias=nmx[h01][:],
                       accum_out=zs[h01][:], inc="a")
                    exp_mark[h01] = mark("a")
                    wait("vector", "a", exp_mark[h01])
                    do("vector", "reciprocal", zr[h01][:], zs[h01][:], inc="v")
                    zrec_v[h01] = mark("v")
                    wait("scalar", "v", zrec_v[h01])
                    do("scalar", "mul", pbuf[h01][:, 0:nk5 * 512],
                       pbuf[h01][:, 0:nk5 * 512], zr[h01][:], inc="a")
                    st["pn_a"] = mark("a")

                def stg3(st):
                    h01, s, nk1 = st["h01"], st["s"], st["nk1"]
                    wait("sync", "a", st["pn_a"])
                    wait("sync", "p", omm_mark[h01])
                    pt_tok[h01] = P.dma(
                        "sync",
                        ptb[h01][:].rearrange("p (k f) -> p k f", f=128)[:, 0:nk1, :],
                        pbuf[h01][:].rearrange("p (k f) -> p k f", f=128)[:, 0:nk1, :],
                        meth="dma_start_transpose")
                    pt3 = ptb[h01][:].rearrange("p (k f) -> p k f", f=128)
                    P.wait_t("tensor", pt_tok[h01])
                    wait("tensor", "v", v_ready[st["p"]])
                    wait("tensor", *bank_free[6 + h01])
                    for c1 in range(nk1):
                        do("tensor", "matmul", banks[6 + h01][:, 0:128],
                           vp[:, c1 * 256 + h01 * 128:c1 * 256 + h01 * 128 + 128],
                           pt3[:, c1, :],
                           start=(c1 == 0), stop=(c1 == nk1 - 1),
                           inc=("p" if c1 == nk1 - 1 else None))
                    omm_mark[h01] = mark("p")
                    wait("vector", "p", omm_mark[h01])
                    P.wait_t("vector", o_war[h01])
                    do("vector", "tensor_copy", osb[h01][:, s * 128:(s + 1) * 128],
                       banks[6 + h01][:, 0:128], inc="v")
                    bank_free[6 + h01] = ("v", mark("v"))
                    if st["stp"] == NSLOT - 1:
                        wait("gpsimd", "v", bank_free[6 + h01][1])
                        o_war[h01] = P.dma("gpsimd", o_scr[2 * st["p"] + h01],
                                           osb[h01][:])

                steps_all = []
                for p in range(npairs):
                    for stp in range(NSLOT):
                        for h01 in ((0, 1) if p % 2 == 0 else (1, 0)):
                            up = (h01 == 0) == (p % 2 == 0)
                            s = stp if up else NSLOT - 1 - stp
                            nk5 = CAPS512[s]
                            steps_all.append(dict(
                                p=p, h01=h01, s=s, stp=stp, nk5=nk5,
                                nk1=CAPS128[s],
                                so=0 if h01 == 0 else (9 - nk5) * 512))
                T = len(steps_all)
                for g in range(T + 2):
                    if g < T and g % 16 == 0:
                        emit_qkv_A(g // 16)
                        if g == 0:
                            emit_qkv_V(0)
                        wait("tensor", "v", kn_ready[g // 16])
                    b0 = g % 16 == 0
                    b1 = g % 16 == 1
                    if b1 and 2 <= g:
                        stg3(steps_all[g - 2])
                    if b0 and 1 <= g <= T:
                        stg2(steps_all[g - 1])
                    if g < T:
                        stg1(steps_all[g])
                    if (not b0) and 1 <= g <= T:
                        stg2(steps_all[g - 1])
                    if (not b1) and 2 <= g:
                        stg3(steps_all[g - 2])
                    if 2 <= g and (g - 2) % 16 == 15 and (g - 2) // 16 + 1 < npairs:
                        emit_qkv_V((g - 2) // 16 + 1)
            P.barrier()
            es2.close()
            es_mid.close()
            if debug >= 2:
                P.dma("gpsimd", dbg["d_o"], o_scr)

        if debug == 1:
            es_mid.close()
            zt = sb("dbg_zero", [128, D], F32, es_root)
            zq = do("gpsimd", "memset", zt[:], 0.0, inc="q")
            wait("gpsimd", "q", zq)
            for t in range(NSLOT):
                P.dma("gpsimd", out[t * 128:(t + 1) * 128, :], zt[:])
        else:
            es3 = ExitStack()
            hs = [sb(f"p3_hs{i}", [112, 1024], BF16, es3) for i in range(2)]
            es3a = ExitStack()
            at_sb = sb("p3_at", [128, 32 * 224], BF16, es3a)
            gwt_sb = sb("p3_gwt", [128, 32 * 28], BF16, es3a)
            e_sb = sb("p3_esel", [NEXP, 224], F32, es3a)
            xk = [sb(f"p3_xk{i}", [128, 1024], BF16, es3a) for i in range(2)]
            lg_sb = sb("p3_lg", [128, 224], F32, es3a)
            gate_sb = sb("p3_gate", [128, 224], F32, es3a)
            v1s = sb("p3_v1", [128, 1], F32, es3a)
            v2s = sb("p3_v2", [128, 1], F32, es3a)
            is1 = sb("p3_is1", [128, 28], F32, es3a)
            is2 = sb("p3_is2", [128, 28], F32, es3a)
            mskd = sb("p3_mskd", [128, 28], F32, es3a)
            tmpa = sb("p3_tmpa", [128, 28], F32, es3a)
            p1t = sb("p3_p1t", [128, 1], F32, es3a)
            p1o = sb("p3_p1o", [128, 1], F32, es3a)
            p2o = sb("p3_p2o", [128, 1], F32, es3a)
            gate_t = sb("p3_gatet", [NEXP, 1024], F32, es3a)
            gmat = [sb(f"p3_gm{i}", [112, 1024], F32, es3a) for i in range(2)]
            ps_l = ps("ps_l", es3a)
            ps_h = [ps(f"ps_h{i}", es3a) for i in range(4)]
            ps_g = [ps(f"ps_g{i}", es3a) for i in range(2)]

            t_at = P.dma("gpsimd", at_sb[:].rearrange("p (k f) -> p k f", k=32),
                         a_t.rearrange("(k p) f -> p k f", p=128))
            t_gw = P.dma("gpsimd", gwt_sb[:].rearrange("p (k f) -> p k f", k=32),
                         gate_wt.rearrange("(k p) f -> p k f", p=128))
            t_es = P.dma("gpsimd", e_sb[:], esel_in)
            P.wait_t("tensor", t_at)
            P.wait_t("tensor", t_gw)
            at3 = at_sb[:].rearrange("p (k f) -> p k f", f=224)
            gw3 = gwt_sb[:].rearrange("p (k f) -> p k f", f=28)
            xk_used = [0, 0]
            for kc in range(32):
                par = kc % 2
                wait("gpsimd", "p", xk_used[par])
                txk = P.dma("gpsimd", xk[par][:], xnT_scr[kc])
                P.wait_t("tensor", txk)
                for tch in range(8):
                    do("tensor", "matmul", ps_l[:, tch * 28:(tch + 1) * 28],
                       xk[par][:, tch * 128:(tch + 1) * 128], gw3[:, kc, :],
                       start=(kc == 0 and tch == 0), stop=(kc == 31),
                       skip_group_check=True)
                for er in range(2):
                    for half in range(2):
                        do("tensor", "matmul", ps_h[er * 2 + half][0:112, :],
                           at3[:, kc, er * 112:(er + 1) * 112],
                           xk[par][:, half * 512:(half + 1) * 512],
                           start=(kc == 0), stop=(kc == 31),
                           skip_group_check=True,
                           inc=("p" if (er == 1 and half == 1) else None))
                xk_used[par] = mark("p")
            h_done = mark("p")
            wait("vector", "p", h_done)
            do("vector", "tensor_copy", lg_sb[:], ps_l[:, 0:224], inc="v")
            wait("vector", "v", mark("v"))
            gt_marks = []
            for tch in range(8):
                lo = lg_sb[:, tch * 28:(tch + 1) * 28]
                do("vector", "reduce_max", v1s[:], lo, axis=AX.X, inc="v")
                wait("vector", "v", mark("v"))
                do("vector", "tensor_single_scalar", is1[:], lo, v1s[:],
                   ALU.is_ge, inc="v")
                wait("vector", "v", mark("v"))
                do("vector", "scalar_tensor_tensor", mskd[:], is1[:], -1e30,
                   lo, ALU.mult, ALU.add, inc="v")
                wait("vector", "v", mark("v"))
                do("vector", "reduce_max", v2s[:], mskd[:], axis=AX.X, inc="v")
                wait("vector", "v", mark("v"))
                do("vector", "tensor_single_scalar", is2[:], mskd[:], v2s[:],
                   ALU.is_ge, inc="v")
                do("vector", "tensor_sub", p1t[:], v1s[:], v2s[:], inc="v")
                vps = mark("v")
                wait("scalar", "v", vps)
                do("scalar", "activation", p1o[:], p1t[:], AF.Sigmoid, inc="a")
                aps = mark("a")
                wait("vector", "a", aps)
                do("vector", "tensor_scalar", p2o[:], p1o[:], -1.0, 1.0,
                   ALU.mult, op1=ALU.add, inc="v")
                do("vector", "tensor_scalar", tmpa[:], is1[:], p1o[:], None,
                   ALU.mult, inc="v")
                wait("vector", "v", mark("v"))
                do("vector", "tensor_scalar", is2[:], is2[:], p2o[:], None,
                   ALU.mult, inc="v")
                wait("vector", "v", mark("v"))
                do("vector", "tensor_add", gate_sb[:, tch * 28:(tch + 1) * 28],
                   tmpa[:], is2[:], inc="v")
                gv = mark("v")
                wait("tensor", "v", gv)
                do("tensor", "transpose",
                   ps_g[tch // 4][0:NEXP, (tch % 4) * 128:(tch % 4) * 128 + 128],
                   gate_sb[:, tch * 28:(tch + 1) * 28], g_ident[:],
                   inc="p")
                gt_marks.append(mark("p"))
            for gb in range(2):
                wait("vector", "p", gt_marks[gb * 4 + 3])
                do("vector", "tensor_copy", gate_t[:, gb * 512:(gb + 1) * 512],
                   ps_g[gb][0:NEXP, :], inc="v")
            gt_v = mark("v")
            if debug >= 3:
                wait("gpsimd", "v", gt_v)
                P.dma("gpsimd", dbg["d_lg"], lg_sb[:])
                P.dma("gpsimd", dbg["d_gate"], gate_sb[:])
                P.dma("gpsimd", dbg["d_gatet"], gate_t[:])
            P.wait_t("tensor", t_es)
            wait("tensor", "v", gt_v)
            gm_marks = []
            for er in range(2):
                for half in range(2):
                    if er == 1:
                        wait("tensor", "v", gm_marks[half])
                    do("tensor", "matmul", ps_g[half][0:112, :],
                       e_sb[:, er * 112:(er + 1) * 112],
                       gate_t[:, half * 512:(half + 1) * 512],
                       start=True, stop=True, inc="p")
                    gmm = mark("p")
                    wait("vector", "p", gmm)
                    do("vector", "tensor_copy",
                       gmat[er][:, half * 512:(half + 1) * 512],
                       ps_g[half][0:112, :], inc="v")
                    gm_marks.append(mark("v"))
            for er in range(2):
                for half in range(2):
                    wait("vector", "v", gm_marks[er * 2 + half])
                    do("vector", "tensor_mul", hs[er][:, half * 512:(half + 1) * 512],
                       ps_h[er * 2 + half][0:112, :],
                       gmat[er][:, half * 512:(half + 1) * 512], inc="v")
            hs_v = mark("v")
            if debug >= 3:
                wait("gpsimd", "v", hs_v)
                P.dma("gpsimd", dbg["d_hs"][0:112, :], hs[0][:])
                P.dma("gpsimd", dbg["d_hs"][112:224, :], hs[1][:])
            P.barrier()
            es3a.close()

            es3b = ExitStack()
            o_full = sb("p3_of", [128, 32 * 1024], BF16, es3b)
            xn_full = sb("p3_xf", [128, 32 * 1024], BF16, es3b)
            wo_t = [sb(f"p3_wt{i}", [128, 512], BF16, es3b) for i in range(4)]
            bw_t = [sb(f"p3_bw{i}", [128, 512], BF16, es3b) for i in range(4)]
            bt_sb = [sb(f"p3_bt{i}", [112, 4096], BF16, es3b) for i in range(2)]
            xslab = [sb(f"p3_xs{i}", [128, 8 * 512], F32, es3b) for i in range(2)]
            out_st = [sb(f"p3_os{i}", [128, 512], F32, es3b) for i in range(2)]
            ps_acc = [ps(f"ps_acc{i}", es3b) for i in range(8)]

            t_of = P.dma("gpsimd", o_full[:].rearrange("p (k f) -> p k f", k=32),
                         o_scr.rearrange("h p f -> p h f"))
            t_xf = P.dma("gpsimd", xn_full[:].rearrange("p (k f) -> p k f", k=32),
                         xnT_scr.rearrange("h p f -> p h f"))
            tb0 = P.dma("gpsimd", bt_sb[0][:], b_t[0:112, :])
            tb1 = P.dma("gpsimd", bt_sb[1][:], b_t[112:224, :])
            for tk in (t_of, t_xf, tb0, tb1):
                P.wait_t("tensor", tk)
            of3 = o_full[:].rearrange("p (k f) -> p k f", f=1024)
            xf3 = xn_full[:].rearrange("p (k f) -> p k f", f=1024)
            wt_used = [0, 0, 0, 0]
            xs_used = [0, 0]
            acc_free = [("v", 0)] * 8
            os_war = [None, None]
            for dc in range(8):
                pdc = dc % 2
                wait("gpsimd", "v", xs_used[pdc])
                t_xs = P.dma(
                    "gpsimd",
                    xslab[pdc][:].rearrange("p (k f) -> p k f", k=8),
                    x_own[:, dc * 512:(dc + 1) * 512].rearrange("(k p) f -> p k f", p=128))
                P.wait_t("vector", t_xs)
                for kc in range(32):
                    par = kc % 4
                    wait("gpsimd", "p", wt_used[par])
                    t_wo = P.dma("gpsimd", wo_t[par][:],
                                 wo[kc * 128:(kc + 1) * 128, dc * 512:(dc + 1) * 512])
                    t_bw = P.dma("gpsimd", bw_t[par][:],
                                 base_wt[kc * 128:(kc + 1) * 128, dc * 512:(dc + 1) * 512])
                    P.wait_t("tensor", t_wo)
                    P.wait_t("tensor", t_bw)
                    for qc in range(8):
                        if kc == 0:
                            wait("tensor", *acc_free[qc])
                        do("tensor", "matmul", ps_acc[qc][:],
                           of3[:, kc, qc * 128:(qc + 1) * 128], wo_t[par][:],
                           start=(kc == 0), stop=False)
                    for qc in range(8):
                        do("tensor", "matmul", ps_acc[qc][:],
                           xf3[:, kc, qc * 128:(qc + 1) * 128], bw_t[par][:],
                           start=False, stop=False,
                           inc=("p" if qc == 7 else None))
                    wt_used[par] = mark("p")
                qc_done = []
                for qc in range(8):
                    for er in range(2):
                        do("tensor", "matmul", ps_acc[qc][:],
                           hs[er][:, qc * 128:(qc + 1) * 128],
                           bt_sb[er][:, dc * 512:(dc + 1) * 512],
                           start=False, stop=(er == 1),
                           inc=("p" if (er == 1) else None))
                    qc_done.append(mark("p"))
                for qc in range(8):
                    pq = qc % 2
                    wait("vector", "p", qc_done[qc])
                    P.wait_t("vector", os_war[pq])
                    do("vector", "tensor_add", out_st[pq][:], ps_acc[qc][:],
                       xslab[pdc][:, qc * 512:(qc + 1) * 512], inc="v")
                    acc_free[qc] = ("v", mark("v"))
                    wait("gpsimd", "v", acc_free[qc][1])
                    os_war[pq] = P.dma(
                        "gpsimd",
                        out[qc * 128:(qc + 1) * 128, dc * 512:(dc + 1) * 512],
                        out_st[pq][:])
                xs_used[pdc] = mark("v")
            es3b.close()
            es3.close()

        P.wait_all_dma("gpsimd", "gpsimd")
        P.wait_all_dma("gpsimd", "sync")

        block = es_root.enter_context(nc.Block())
        for eng in ENGS:
            getattr(block, eng)(lambda e, q=P.q[eng]: [f(e) for f in q])
    return nc



def _prep_shared(i):
    perm = np.concatenate([np.arange(0, ROPE, 2), np.arange(1, ROPE, 2)])
    wq_b = np.asarray(i["L0_attn_Wq_b"])
    wqb_p = np.empty((NPAIR, QR, 384), np.float32)
    for p in range(NPAIR):
        h0, h1 = 2 * p, 2 * p + 1
        c0 = wq_b[:, h0 * QD:h0 * QD + NOPE]
        c1 = wq_b[:, h1 * QD:h1 * QD + NOPE]
        p0 = wq_b[:, h0 * QD + NOPE:(h0 + 1) * QD][:, perm]
        p1 = wq_b[:, h1 * QD + NOPE:(h1 + 1) * QD][:, perm]
        wqb_p[p] = np.concatenate([c0, c1, p0, p1], axis=1)
    wkv_a = np.array(i["L0_attn_Wkv_a"])
    wkv_a[:, KVR:] = wkv_a[:, KVR:][:, perm]
    wkv_b = np.asarray(i["L0_attn_Wkv_b"])
    wkvb_p = np.empty((NPAIR, KVR, 512), np.float32)
    for p in range(NPAIR):
        h0, h1 = 2 * p, 2 * p + 1
        kn0 = wkv_b[:, h0 * 256:h0 * 256 + NOPE]
        kn1 = wkv_b[:, h1 * 256:h1 * 256 + NOPE]
        v0 = wkv_b[:, h0 * 256 + NOPE:(h0 + 1) * 256]
        v1 = wkv_b[:, h1 * 256 + NOPE:(h1 + 1) * 256]
        wkvb_p[p] = np.concatenate([kn0, kn1, v0, v1], axis=1)
    a_t = np.ascontiguousarray(
        np.asarray(i["L0_ffn_lora_A"]).reshape(NEXP * RANK, D).T)
    b_t = np.ascontiguousarray(
        np.asarray(i["L0_ffn_lora_B"]).transpose(0, 2, 1).reshape(NEXP * RANK, D))
    base_wt = np.ascontiguousarray(np.asarray(i["L0_ffn_base_W"]).T)
    gate_wt = np.ascontiguousarray(np.asarray(i["L0_ffn_gate_W"]).T)
    qlnw = np.ascontiguousarray(
        np.asarray(i["L0_attn_q_a_ln_w"]).reshape(12, 128).T)
    kvlnw = np.ascontiguousarray(
        np.asarray(i["L0_attn_kv_a_ln_w"]).reshape(4, 128).T)

    invf = (1.0 / 10000.0 ** (np.arange(0, ROPE, 2, dtype=np.float64) / ROPE))
    pos = np.arange(S, dtype=np.float64)
    fr = np.outer(invf, pos)
    cosk = np.cos(fr).astype(np.float32)
    sink = np.sin(fr).astype(np.float32)
    cosk = np.concatenate([cosk, cosk], axis=0)
    sink = np.concatenate([sink, sink], axis=0)

    rq = np.zeros((128, 128), np.float32)
    for hb in (0, 64):
        for f in range(32):
            rq[hb + f + 32, hb + f] = -1.0
            rq[hb + f, hb + f + 32] = 1.0
    rk = np.zeros((64, 64), np.float32)
    for f in range(32):
        rk[f + 32, f] = -1.0
        rk[f, f + 32] = 1.0

    ident = np.eye(128, dtype=np.float32)
    esel = np.zeros((NEXP, 224), np.float32)
    for e in range(14):
        esel[e, e * 8:(e + 1) * 8] = LORA_SCALING
    for e in range(14, 28):
        esel[e, 112 + (e - 14) * 8:112 + (e - 13) * 8] = LORA_SCALING

    return dict(
        wq_a=np.ascontiguousarray(np.asarray(i["L0_attn_Wq_a"], np.float32)),
        wq_b=wqb_p, wkv_a=np.ascontiguousarray(wkv_a), wkv_b=wkvb_p,
        wo=np.ascontiguousarray(np.asarray(i["L0_attn_Wo"], np.float32)),
        base_wt=base_wt, gate_wt=gate_wt, a_t=a_t, b_t=b_t,
        qlnw=qlnw, kvlnw=kvlnw, cosk=np.ascontiguousarray(cosk),
        sink=np.ascontiguousarray(sink),
        rq=rq, rk=rk, ident=ident, esel=esel,
    )


def _prep_core(x, shared, j, b):
    tiles = tiles_for(j)
    xb = np.asarray(x[b])
    x_own = np.concatenate([xb[t * 128:(t + 1) * 128] for t in tiles], 0)
    invf = (1.0 / 10000.0 ** (np.arange(0, ROPE, 2, dtype=np.float64) / ROPE))
    posq = np.concatenate(
        [np.arange(t * 128, (t + 1) * 128, dtype=np.float64) for t in tiles])
    fr = np.outer(invf, posq)
    c64 = np.concatenate([np.cos(fr), np.cos(fr)], 0)
    s64 = np.concatenate([np.sin(fr), np.sin(fr)], 0)
    cosq = (np.concatenate([c64, c64], 0) * SCALE).astype(np.float32)
    sinq = (np.concatenate([s64, s64], 0) * SCALE).astype(np.float32)
    masks = np.zeros((128, NSLOT * 512), np.float32)
    for s in range(NSLOT):
        t = tiles[s]
        k0 = CAPS128[s] * 128 - 512
        kk = np.arange(k0, k0 + 512)
        qq = t * 128 + np.arange(128)
        masks[:, s * 512:(s + 1) * 512] = np.where(
            kk[None, :] <= qq[:, None], 0.0, -1e30)
    d = dict(shared)
    d.update(x_full=np.ascontiguousarray(xb), x_own=np.ascontiguousarray(x_own),
             cosq=cosq, sinq=sinq, masks=masks)
    return d


_nc_cache = {}


def _get_nc(debug=0):
    if debug not in _nc_cache:
        _nc_cache[debug] = build_nc(debug)
    return _nc_cache[debug]


def kernel(**inputs):
    shared = _prep_shared(inputs)
    x = np.asarray(inputs["L0_attn_hidden_states"], np.float32)
    in_maps = []
    for c in range(8):
        in_maps.append(_prep_core(x, shared, c % 4, c // 4))
    nc = _get_nc(0)
    res = run_bass_kernel_spmd(nc, in_maps, core_ids=list(range(8)))
    out = np.empty((B, S, D), np.float32)
    for c in range(8):
        tiles = tiles_for(c % 4)
        r = res.results[c]["out"]
        for s, t in enumerate(tiles):
            out[c // 4, t * 128:(t + 1) * 128, :] = r[s * 128:(s + 1) * 128, :]
    return out


# version 13 (final)
# speedup vs baseline: 1.2402x; 1.0385x over previous
import numpy as np
from contextlib import ExitStack
import concourse.bass as bass
import concourse.mybir as mybir
from concourse.bass_utils import run_bass_kernel_spmd

F32 = mybir.dt.float32
BF16 = mybir.dt.bfloat16
AF = mybir.ActivationFunctionType
ALU = mybir.AluOpType
AX = mybir.AxisListType

B, S, D = 2, 4096, 4096
H, QR, KVR = 32, 1536, 512
NOPE, ROPE, VD, QD = 128, 64, 128, 192
NEXP, RANK = 28, 8
LORA_SCALING = 2.0
SCALE = float(QD) ** -0.5
LN_EPS = 1e-5
RMS_EPS = 1e-6
NSLOT = 8
CAPS128 = [32, 28, 24, 20, 16, 12, 8, 4]
CAPS512 = [8, 7, 6, 5, 4, 3, 2, 1]
NT = 32
NPAIR = 16

ENGS = ("tensor", "vector", "scalar", "sync", "gpsimd")


def tiles_for(j):
    return [31 - j, 27 - j, 23 - j, 19 - j, 12 + j, 8 + j, 4 + j, j]


NG = 8
NS = 4


class Prog:

    def __init__(self, nc, es):
        self.nc = nc
        self.q = {k: [] for k in ENGS}
        self.sems = {}
        self.cnt = {}
        keys = (["p", "v", "a", "q", "y"]
                + [f"g{i}" for i in range(NG)] + [f"s{i}" for i in range(NS)])
        for name in keys:
            self.sems[name] = es.enter_context(nc.semaphore("sem_" + name))
            self.cnt[name] = 0
        self.ring = {"gpsimd": [f"g{i}" for i in range(NG)],
                     "sync": [f"s{i}" for i in range(NS)]}
        self.ridx = {"gpsimd": 0, "sync": 0}

    def do(self, eng, meth, *args, inc=None, **kw):
        sem = amt = None
        r = None
        if inc is not None:
            key, amt = inc if isinstance(inc, tuple) else (inc, 1)
            self.cnt[key] += amt
            sem = self.sems[key]
            r = self.cnt[key]

        def f(e, meth=meth, args=args, kw=kw, sem=sem, amt=amt):
            ins = getattr(e, meth)(*args, **kw)
            if sem is not None:
                ins.then_inc(sem, amt)

        self.q[eng].append(f)
        return r

    def dma(self, eng, *args, meth="dma_start", **kw):
        key = self.ring[eng][self.ridx[eng]]
        self.ridx[eng] = (self.ridx[eng] + 1) % len(self.ring[eng])
        if self.cnt[key] > 0:
            self.wait(eng, key, self.cnt[key])
        self.do(eng, meth, *args, inc=(key, 16), **kw)
        return (key, self.cnt[key])

    def wait(self, eng, key, val):
        if val <= 0:
            return
        self.q[eng].append(lambda e, s=self.sems[key], v=val: e.wait_ge(s, v))

    def wait_t(self, eng, tok):
        if tok is not None:
            self.wait(eng, tok[0], tok[1])

    def wait_all_dma(self, eng_wait, eng_ring):
        for key in self.ring[eng_ring]:
            self.wait(eng_wait, key, self.cnt[key])

    def mark(self, key):
        return self.cnt[key]

    def barrier(self):
        keys = (["p", "v", "a", "q"]
                + self.ring["gpsimd"] + self.ring["sync"])
        for eng in ENGS:
            for key in keys:
                self.wait(eng, key, self.cnt[key])


def build_nc(debug=0, p0_groups=8, do_p0c=True, p0_stage=100, npairs=NPAIR):
    nc = bass.Bass("TRN2", target_bir_lowering=False)

    dram = {}

    def din(name, shape, dt=F32):
        dram[name] = nc.dram_tensor(name, shape, dt, kind="ExternalInput").ap()
        return dram[name]

    x_full = din("x_full", [S, D])
    x_own = din("x_own", [NSLOT * 128, D])
    wq_a = din("wq_a", [D, QR])
    wq_b = din("wq_b", [NPAIR, QR, 384])
    wkv_a = din("wkv_a", [D, KVR + ROPE])
    wkv_b = din("wkv_b", [NPAIR, KVR, 512])
    wo = din("wo", [D, D])
    base_wt = din("base_wt", [D, D])
    gate_wt = din("gate_wt", [D, NEXP])
    a_t = din("a_t", [D, NEXP * RANK])
    b_t = din("b_t", [NEXP * RANK, D])
    qlnw = din("qlnw", [128, 12])
    kvlnw = din("kvlnw", [128, 4])
    cosq = din("cosq", [128, 1024])
    sinq = din("sinq", [128, 1024])
    cosk = din("cosk", [64, S])
    sink = din("sink", [64, S])
    masks = din("masks", [128, NSLOT * 512])
    rq_in = din("rq", [128, 128])
    rk_in = din("rk", [64, 64])
    ident_in = din("ident", [128, 128])
    esel_in = din("esel", [NEXP, 224])

    out = nc.dram_tensor("out", [NSLOT * 128, D], F32, kind="ExternalOutput").ap()

    o_scr = nc.dram_tensor("o_scr", [H, 128, 1024], BF16, kind="Internal").ap()
    xnT_scr = nc.dram_tensor("xnT_scr", [32, 128, 1024], BF16, kind="Internal").ap()
    kpe_scr = nc.dram_tensor("kpe_scr", [64, S], BF16, kind="Internal").ap()
    outT_scr = nc.dram_tensor("outT_scr", [D, 1024], F32, kind="Internal").ap()

    dbg = {}
    if debug >= 1:
        for name, shape, dt in [
            ("d_cnorm", [128, 4 * S], BF16),
            ("d_kpe", [64, S], BF16),
            ("d_qnorm", [128, 12 * 1024], BF16),
            ("d_xnT", [32, 128, 1024], BF16),
            ("d_misc", [64, 2048], F32),
        ]:
            dbg[name] = nc.dram_tensor(name, shape, dt, kind="ExternalOutput").ap()
    if debug >= 2:
        dbg["d_o"] = nc.dram_tensor("d_o", [H, 128, 1024], BF16, kind="ExternalOutput").ap()
    if debug >= 3:
        dbg["d_hs"] = nc.dram_tensor("d_hs", [224, 1024], BF16, kind="ExternalOutput").ap()
        dbg["d_outT"] = nc.dram_tensor("d_outT", [D, 1024], F32, kind="ExternalOutput").ap()
        dbg["d_lg"] = nc.dram_tensor("d_lg", [128, 224], F32, kind="ExternalOutput").ap()
        dbg["d_gate"] = nc.dram_tensor("d_gate", [128, 224], F32, kind="ExternalOutput").ap()
        dbg["d_gatet"] = nc.dram_tensor("d_gatet", [28, 1024], F32, kind="ExternalOutput").ap()

    es_root = ExitStack()
    with es_root:
        P = Prog(nc, es_root)
        do, wait, mark = P.do, P.wait, P.mark

        def sb(name, shape, dt, stack):
            return stack.enter_context(nc.sbuf_tensor(name, shape, dt))

        def ps(name, stack, shape=None):
            return stack.enter_context(
                nc.psum_tensor(name, shape or [128, 512], F32))

        g_ident = sb("g_ident", [128, 128], F32, es_root)
        g_ones_col = sb("g_ones_col", [128, 1], F32, es_root)
        g_ones_row = sb("g_ones_row", [1, 128], F32, es_root)
        g_qlnw = sb("g_qlnw", [128, 12], F32, es_root)
        g_kvlnw = sb("g_kvlnw", [128, 4], F32, es_root)
        es_mid = ExitStack()
        g_cnorm = [sb(f"g_cnorm{i}", [128, S], BF16, es_mid) for i in range(4)]
        g_kpe = sb("g_kpe", [128, S], BF16, es_mid)
        g_qnorm = sb("g_qnorm", [128, 12 * 1024], BF16, es_mid)

        tok_con = [P.dma("gpsimd", dst[:], src) for dst, src in
                   [(g_ident, ident_in), (g_qlnw, qlnw), (g_kvlnw, kvlnw)]]
        do("gpsimd", "memset", g_ones_col[:], 1.0, inc="q")
        do("gpsimd", "memset", g_ones_row[:], 1.0, inc="q")
        c_consts_q = mark("q")

        es0 = ExitStack()
        with es0:
            x_sb = [sb(f"p0_x{i}", [128, D], BF16, es0) for i in range(2)]
            xn_sb = [sb(f"p0_xn{i}", [128, D], BF16, es0) for i in range(2)]
            xnT_h = [sb(f"p0_xnTh{i}", [128, 32 * 256], BF16, es0)
                     for i in range(2)]
            wkva_sb = sb("p0_wkva", [128, 32 * 576], BF16, es0)

            rk_sb = sb("p0_rk", [64, 64], BF16, es0)
            c_f32 = sb("p0_cf", [128, 4 * 512], F32, es0)
            sq_f32 = sb("p0_sq", [128, 512], F32, es0)
            kpe_f = sb("p0_kpef", [64, 512], F32, es0)
            kpe_bf = sb("p0_kpebf", [64, 512], BF16, es0)
            kt1 = sb("p0_kt1", [64, 512], F32, es0)
            kt2 = sb("p0_kt2", [64, 512], F32, es0)
            cosf = sb("p0_cosf", [64, 512], F32, es0)
            sinf = sb("p0_sinf", [64, 512], F32, es0)
            bnb = [sb(f"p0_bn{i}", [128, 8 * 6], F32, es0) for i in range(2)]
            mv = [sb(f"p0_mv{i}", [128, 2], F32, es0) for i in range(2)]
            rstd = [sb(f"p0_rstd{i}", [128, 1], F32, es0) for i in range(2)]
            nmr = [sb(f"p0_nmr{i}", [128, 1], F32, es0) for i in range(2)]
            ssr = sb("p0_ssr", [1, 512], F32, es0)
            rsr = sb("p0_rsr", [1, 512], F32, es0)
            rbc = sb("p0_rbc", [128, 512], F32, es0)

            ps_ck = [ps(f"ps_ck{i}", es0) for i in range(5)]
            ps_ss = ps("ps_ss", es0)
            ps_bc = ps("ps_bc", es0)
            ps_rk = ps("ps_rk", es0)

            tok_p0 = [
                P.dma("gpsimd",
                      wkva_sb[:].rearrange("p (k f) -> p k f", k=32),
                      wkv_a.rearrange("(k p) f -> p k f", p=128)),
                P.dma("gpsimd", rk_sb[:], rk_in),
            ]

            ln_state = {}

            def emit_ln(par):
                for i in range(8):
                    do("vector", "bn_stats", bnb[par][:].rearrange("p (n s) -> p n s", s=6)[:, i, :],
                       x_sb[par][:, i * 512:(i + 1) * 512],
                       inc=("v" if i == 7 else None))
                wait("vector", "v", mark("v"))
                do("vector", "bn_aggr", mv[par][:],
                   bnb[par][:].rearrange("p (n s) -> p n s", s=6), inc="v")
                vagg = mark("v")
                wait("vector", "v", vagg)
                do("vector", "tensor_scalar_add", rstd[par][:], mv[par][:, 1:2],
                   LN_EPS, inc="v")
                vagg = mark("v")
                wait("scalar", "v", vagg)
                do("scalar", "activation", rstd[par][:], rstd[par][:], AF.Ln, inc="a")
                a1 = mark("a")
                wait("scalar", "a", a1)
                do("scalar", "activation", rstd[par][:], rstd[par][:], AF.Exp,
                   scale=-0.5, inc="a")
                a2 = mark("a")
                do("vector", "tensor_scalar_mul", nmr[par][:], mv[par][:, 0:1],
                   -1.0, inc="v")
                vn = mark("v")
                wait("vector", "v", vn)
                wait("vector", "a", a2)
                do("vector", "tensor_scalar", xn_sb[par][:], x_sb[par][:],
                   nmr[par][:], rstd[par][:], ALU.add, op1=ALU.mult, inc="v")
                return mark("v")

            tr_marks = [None, None]
            ck_evac = [0, 0, 0, 0, 0]
            xnT_war = [0, 0]
            prev_rope_v = 0
            prev_ssr_v = 0
            prev_kpe_a = 0
            prev_prk = 0
            odd_marks = []
            for g in range(p0_groups):
                for tt in range(4):
                    t = g * 4 + tt
                    par = t % 2
                    if t >= 2:
                        wait("gpsimd", "v", ln_state[par])
                    gl = P.dma("gpsimd", x_sb[par][:],
                               x_full[t * 128:(t + 1) * 128, :])
                    P.wait_t("vector", gl)
                    P.wait_t("vector", tr_marks[par])
                    ln_state[par] = emit_ln(par)
                    if p0_stage < 2:
                        continue
                    wait("sync", "v", ln_state[par])
                    hb = (t // 2) % 2
                    if t % 2 == 0:
                        wait("sync", "p", xnT_war[hb])
                    tr_marks[par] = P.dma(
                        "sync",
                        xnT_h[hb][:].rearrange("p (k f) -> p k f", f=256)[:, :, (t % 2) * 128:(t % 2) * 128 + 128],
                        xn_sb[par][:].rearrange("p (k f) -> p k f", f=128),
                        meth="dma_start_transpose")
                    if t % 2 == 1:
                        hg_odd = (tt == 3)
                        P.wait_t("tensor", tr_marks[0])
                        P.wait_t("tensor", tr_marks[1])
                        if t == 1:
                            for tk in tok_p0 + tok_con:
                                P.wait_t("tensor", tk)
                            wait("tensor", "q", c_consts_q)
                        if not hg_odd:
                            wait("tensor", "a", prev_kpe_a)
                        xnT3 = xnT_h[hb][:].rearrange("p (k f) -> p k f", f=256)
                        wk3 = wkva_sb[:].rearrange("p (k f) -> p k f", f=576)
                        col = slice(0, 256) if not hg_odd else slice(256, 512)
                        if hg_odd:
                            odd_marks.clear()
                        for mc in range(5 if p0_stage >= 40 else 4):
                            pb = ps_ck[mc]
                            if not hg_odd:
                                wait("tensor", "v", ck_evac[mc])
                            lo = mc * 128
                            hi = min(lo + 128, 576)
                            for kc in range(32):
                                do("tensor", "matmul",
                                   pb[0:hi - lo, col], wk3[:, kc, lo:hi],
                                   xnT3[:, kc, :],
                                   start=(kc == 0 and not hg_odd),
                                   stop=(kc == 31),
                                   skip_group_check=True,
                                   inc=("p" if kc == 31 else None))
                            if hg_odd:
                                odd_marks.append(mark("p"))
                        xnT_war[hb] = mark("p")
                if p0_stage < 3:
                    continue
                ssmm = [0, 0]
                first_ss = True
                for mc in range(5 if p0_stage >= 40 else 4):
                    pb = ps_ck[mc]
                    pm = odd_marks[mc]
                    if mc < 4:
                        sqs = sq_f32[:]
                        wait("vector", "p", pm)
                        do("vector", "tensor_copy",
                           c_f32[:, mc * 512:(mc + 1) * 512], pb[:], inc="v")
                        cv = mark("v")
                        wait("vector", "v", cv)
                        wait("vector", "p", ssmm[0])
                        do("vector", "tensor_mul", sqs,
                           c_f32[:, mc * 512:(mc + 1) * 512],
                           c_f32[:, mc * 512:(mc + 1) * 512], inc="v")
                        ck_evac[mc] = mark("v")
                        wait("tensor", "v", ck_evac[mc])
                        if first_ss:
                            wait("tensor", "v", prev_ssr_v)
                            first_ss = False
                        do("tensor", "matmul", ps_ss[0:1, :], g_ones_col[:], sqs,
                           start=(mc == 0), stop=(mc == 3), inc="p")
                        ssmm[0] = mark("p")
                    else:
                        wait("vector", "p", pm)
                        do("vector", "tensor_copy", kpe_f[:], pb[0:64, :], inc="v")
                        kv_m = mark("v")
                        wait("scalar", "v", kv_m)
                        wait("scalar", "p", prev_prk)
                        do("scalar", "activation", kpe_bf[:], kpe_f[:], AF.Copy, inc="a")
                        ck_evac[mc] = mark("v")
                kpe_vm, kpe_am = mark("v"), mark("a")
                pss = ssmm[0]
                if p0_stage < 4:
                    continue
                wait("vector", "p", pss)
                do("vector", "tensor_scalar", ssr[:], ps_ss[0:1, :], 1.0 / KVR,
                   RMS_EPS, ALU.mult, op1=ALU.add, inc="v")
                vss = mark("v")
                prev_ssr_v = vss
                wait("scalar", "v", vss)
                do("scalar", "activation", rsr[:], ssr[:], AF.Ln, inc="a")
                am = mark("a")
                wait("scalar", "a", am)
                do("scalar", "activation", rsr[:], rsr[:], AF.Exp, scale=-0.5, inc="a")
                am = mark("a")
                wait("tensor", "a", am)
                do("tensor", "matmul", ps_bc[:], g_ones_row[:], rsr[:],
                   start=True, stop=True, inc="p")
                pbc = mark("p")
                wait("vector", "p", pbc)
                do("vector", "tensor_copy", rbc[:], ps_bc[:], inc="v")
                vbc = mark("v")
                wait("vector", "v", vbc)
                for i in range(4):
                    do("vector", "scalar_tensor_tensor",
                       g_cnorm[i][:, g * 512:(g + 1) * 512],
                       c_f32[:, i * 512:(i + 1) * 512],
                       g_kvlnw[:, i:i + 1], rbc[:], ALU.mult, ALU.mult, inc="v")
                cn_done = mark("v")
                if p0_stage < 42:
                    prev_ssr_v = vss
                    continue
                wait("tensor", "a", kpe_am)
                wait("tensor", "v", prev_rope_v)
                do("tensor", "matmul", ps_rk[0:64, :], rk_sb[:], kpe_bf[:],
                   start=True, stop=True, inc="p")
                prk = mark("p")
                prev_prk = prk
                if p0_stage < 43:
                    continue
                wait("gpsimd", "v", prev_rope_v)
                ck_tok = P.dma("gpsimd", cosf[:], cosk[:, g * 512:(g + 1) * 512])
                sk_tok = P.dma("gpsimd", sinf[:], sink[:, g * 512:(g + 1) * 512])
                P.wait_t("vector", ck_tok)
                P.wait_t("vector", sk_tok)
                wait("vector", "v", kpe_vm)
                do("vector", "tensor_mul", kt1[:], kpe_f[:], cosf[:], inc="v")
                wait("vector", "p", prk)
                do("vector", "tensor_mul", kt2[:], ps_rk[0:64, :], sinf[:], inc="v")
                v6 = mark("v")
                if p0_stage < 44:
                    prev_rope_v = mark("v")
                    continue
                wait("vector", "v", v6)
                do("vector", "tensor_add", g_kpe[0:64, g * 512:(g + 1) * 512],
                   kt1[:], kt2[:], inc="v")
                prev_rope_v = mark("v")
                prev_kpe_a = kpe_am
                if debug >= 1 and g == 7:
                    wait("gpsimd", "v", prev_rope_v)
                    P.dma("gpsimd", dbg["d_misc"][:, 0:512], kpe_f[:])
                    P.dma("gpsimd", dbg["d_misc"][:, 512:1024], kt1[:])
                    P.dma("gpsimd", dbg["d_misc"][:, 1024:1536], kt2[:])
                    P.dma("gpsimd", dbg["d_misc"][:, 1536:2048], cosf[:])

            p0b_v, p0b_a = mark("v"), mark("a")

        P.barrier()
        if not do_p0c:
            es_mid.close()
            zt = sb("dbg_zero", [128, D], F32, es_root)
            zq = do("gpsimd", "memset", zt[:], 0.0, inc="q")
            wait("gpsimd", "q", zq)
            for t in range(NSLOT):
                P.dma("gpsimd", out[t * 128:(t + 1) * 128, :], zt[:])
            P.wait_all_dma("gpsimd", "gpsimd")
            P.wait_all_dma("gpsimd", "sync")
            block = es_root.enter_context(nc.Block())
            for eng in ENGS:
                getattr(block, eng)(lambda e, q=P.q[eng]: [f(e) for f in q])
            return nc
        es1 = ExitStack()
        with es1:
            x_sb = [sb(f"p1_x{i}", [128, D], BF16, es1) for i in range(2)]
            xn_sb = [sb(f"p1_xn{i}", [128, D], BF16, es1) for i in range(2)]
            xnT_own = sb("p1_xnTown", [128, 32 * 1024], BF16, es1)
            wqa_col = [sb(f"p1_wqa{i}", [128, 32 * 128], BF16, es1) for i in range(2)]
            bnb = [sb(f"p1_bn{i}", [128, 8 * 6], F32, es1) for i in range(2)]
            mv = [sb(f"p1_mv{i}", [128, 2], F32, es1) for i in range(2)]
            rstd = [sb(f"p1_rstd{i}", [128, 1], F32, es1) for i in range(2)]
            nmr = [sb(f"p1_nmr{i}", [128, 1], F32, es1) for i in range(2)]
            qsq = sb("p1_qsq", [128, 512], F32, es1)
            qssr = sb("p1_qssr", [1, 1024], F32, es1)
            qrsr = sb("p1_qrsr", [1, 1024], F32, es1)
            qrbc = sb("p1_qrbc", [128, 1024], F32, es1)
            ps_q = [ps(f"ps_q{i}", es1) for i in range(2)]
            ps_qs = [ps(f"ps_qs{i}", es1) for i in range(2)]
            ps_qb = [ps(f"ps_qb{i}", es1) for i in range(2)]

            tr_marks = [None, None]
            ln2 = {}
            for t in range(8):
                par = t % 2
                if t >= 2:
                    wait("gpsimd", "v", ln2[par])
                gl = P.dma("gpsimd", x_sb[par][:],
                           x_own[t * 128:(t + 1) * 128, :])
                P.wait_t("vector", gl)
                P.wait_t("vector", tr_marks[par])
                for i in range(8):
                    do("vector", "bn_stats",
                       bnb[par][:].rearrange("p (n s) -> p n s", s=6)[:, i, :],
                       x_sb[par][:, i * 512:(i + 1) * 512],
                       inc=("v" if i == 7 else None))
                wait("vector", "v", mark("v"))
                do("vector", "bn_aggr", mv[par][:],
                   bnb[par][:].rearrange("p (n s) -> p n s", s=6), inc="v")
                wait("vector", "v", mark("v"))
                do("vector", "tensor_scalar_add", rstd[par][:], mv[par][:, 1:2],
                   LN_EPS, inc="v")
                wait("scalar", "v", mark("v"))
                do("scalar", "activation", rstd[par][:], rstd[par][:], AF.Ln, inc="a")
                wait("scalar", "a", mark("a"))
                do("scalar", "activation", rstd[par][:], rstd[par][:], AF.Exp,
                   scale=-0.5, inc="a")
                aexp = mark("a")
                do("vector", "tensor_scalar_mul", nmr[par][:], mv[par][:, 0:1],
                   -1.0, inc="v")
                wait("vector", "v", mark("v"))
                wait("vector", "a", aexp)
                do("vector", "tensor_scalar", xn_sb[par][:], x_sb[par][:],
                   nmr[par][:], rstd[par][:], ALU.add, op1=ALU.mult, inc="v")
                ln2[par] = mark("v")
                wait("sync", "v", ln2[par])
                tr_marks[par] = P.dma(
                    "sync",
                    xnT_own[:].rearrange("p (k f) -> p k f", f=1024)[:, :, t * 128:(t + 1) * 128],
                    xn_sb[par][:].rearrange("p (k f) -> p k f", f=128),
                    meth="dma_start_transpose")
            xnT_toks = list(tr_marks)
            for tk in xnT_toks:
                P.wait_t("gpsimd", tk)
            P.dma("gpsimd",
                  xnT_scr.rearrange("k p f -> p k f"),
                  xnT_own[:].rearrange("p (k f) -> p k f", f=1024))
            if debug >= 1:
                P.dma("gpsimd", dbg["d_xnT"].rearrange("k p f -> p k f"),
                      xnT_own[:].rearrange("p (k f) -> p k f", f=1024))
                wait("gpsimd", "v", p0b_v)
                for i in range(4):
                    P.dma("gpsimd", dbg["d_cnorm"][:, i * S:(i + 1) * S],
                          g_cnorm[i][:])
                P.dma("gpsimd", dbg["d_kpe"], g_kpe[0:64, :])

            xo3 = xnT_own[:].rearrange("p (k f) -> p k f", f=1024)
            wq3 = [w[:].rearrange("p (k f) -> p k f", f=128) for w in wqa_col]
            qevac = [0, 0]
            qld_used = [0, 0]
            sq_done = 0
            for mc in range(12):
                par = mc % 2
                if mc >= 2:
                    wait("gpsimd", "p", qld_used[par])
                gl = P.dma("gpsimd",
                           wq3[par],
                           wq_a[:, mc * 128:(mc + 1) * 128].rearrange("(k p) f -> p k f", p=128))
                P.wait_t("tensor", gl)
                if mc == 0:
                    for tk in xnT_toks:
                        P.wait_t("tensor", tk)
                for half in range(2):
                    pb = ps_q[half]
                    wait("tensor", "v", qevac[half])
                    for kc in range(32):
                        do("tensor", "matmul", pb[:],
                           wq3[par][:, kc, :], xo3[:, kc, half * 512:(half + 1) * 512],
                           start=(kc == 0), stop=(kc == 31),
                           inc=("p" if kc == 31 else None))
                    pm = mark("p")
                    qld_used[par] = pm
                    wait("vector", "p", pm)
                    do("vector", "tensor_copy",
                       g_qnorm[:].rearrange("p (k f) -> p k f", f=1024)[:, mc, half * 512:(half + 1) * 512],
                       pb[:], inc="v")
                    vc = mark("v")
                    qnc = g_qnorm[:].rearrange("p (k f) -> p k f", f=1024)[
                        :, mc, half * 512:(half + 1) * 512]
                    wait("vector", "v", vc)
                    do("vector", "tensor_mul", qsq[:], qnc, qnc, inc="v")
                    vs = mark("v")
                    wait("tensor", "v", vs)
                    do("tensor", "matmul", ps_qs[half][0:1, :], g_ones_col[:], qsq[:],
                       start=(mc == 0), stop=(mc == 11), inc="p")
                    pm2 = mark("p")
                    qevac[half] = vs
                    sq_done = pm2
                    wait("vector", "p", sq_done)
            wait("vector", "p", sq_done)
            do("vector", "tensor_scalar", qssr[0:1, 0:512], ps_qs[0][0:1, :],
               1.0 / QR, RMS_EPS, ALU.mult, op1=ALU.add, inc="v")
            do("vector", "tensor_scalar", qssr[0:1, 512:1024], ps_qs[1][0:1, :],
               1.0 / QR, RMS_EPS, ALU.mult, op1=ALU.add, inc="v")
            vq = mark("v")
            wait("scalar", "v", vq)
            do("scalar", "activation", qrsr[:], qssr[:], AF.Ln, inc="a")
            wait("scalar", "a", mark("a"))
            do("scalar", "activation", qrsr[:], qrsr[:], AF.Exp, scale=-0.5, inc="a")
            aq = mark("a")
            wait("tensor", "a", aq)
            do("tensor", "matmul", ps_qb[0][:], g_ones_row[:], qrsr[0:1, 0:512],
               start=True, stop=True)
            do("tensor", "matmul", ps_qb[1][:], g_ones_row[:], qrsr[0:1, 512:1024],
               start=True, stop=True, inc="p")
            pqb = mark("p")
            wait("vector", "p", pqb)
            do("vector", "tensor_copy", qrbc[:, 0:512], ps_qb[0][:])
            do("vector", "tensor_copy", qrbc[:, 512:1024], ps_qb[1][:], inc="v")
            vb = mark("v")
            wait("vector", "v", vb)
            for mc in range(12):
                qn = g_qnorm[:].rearrange("p (k f) -> p k f", f=1024)[:, mc, :]
                do("vector", "scalar_tensor_tensor", qn, qn,
                   g_qlnw[:, mc:mc + 1], qrbc[:], ALU.mult, ALU.mult, inc="v")
            qnorm_done = mark("v")
            if debug >= 1:
                wait("gpsimd", "v", qnorm_done)
                P.dma("gpsimd", dbg["d_qnorm"], g_qnorm[:])

        P.barrier()

        if debug != 1:
            tk1 = P.dma("gpsimd", kpe_scr, g_kpe[0:64, :])
            P.wait_t("gpsimd", tk1)
            tk2 = P.dma("gpsimd", g_kpe[64:128, :], kpe_scr)
            P.wait_t("gpsimd", tk2)

            es2 = ExitStack()
            wqb = sb("p2_wqb", [128, 12 * 384], BF16, es2)
            wkvb = sb("p2_wkvb", [128, 4 * 512], BF16, es2)
            qn_ = [sb(f"p2_qn{i}", [128, 1024], BF16, es2) for i in range(2)]
            qpe = sb("p2_qpe", [128, 1024], BF16, es2)
            xpf = sb("p2_xpf", [128, 1024], F32, es2)
            xpb = sb("p2_xpb", [128, 1024], BF16, es2)
            t1f = sb("p2_t1f", [128, 1024], F32, es2)
            t2f = sb("p2_t2f", [128, 1024], F32, es2)
            kn = [sb(f"p2_kn{i}", [128, S], BF16, es2) for i in range(2)]
            vp = sb("p2_vp", [128, 32 * 256], BF16, es2)
            sco = sb("p2_sco", [128, 4608], F32, es2)
            cmx = [sb(f"p2_cm{i}", [128, 8], F32, es2) for i in range(2)]
            pbuf = [sb(f"p2_pb{i}", [128, 4096], BF16, es2) for i in range(2)]
            ptb = [sb(f"p2_pt{i}", [128, 4096], BF16, es2) for i in range(2)]
            osb = [sb(f"p2_o{i}", [128, 1024], BF16, es2) for i in range(2)]
            nmx = [sb(f"p2_nm{i}", [128, 1], F32, es2) for i in range(2)]
            zs = [sb(f"p2_zs{i}", [128, 1], F32, es2) for i in range(2)]
            zr = [sb(f"p2_zr{i}", [128, 1], F32, es2) for i in range(2)]
            cosq_sb = sb("p2_cosq", [128, 1024], F32, es2)
            sinq_sb = sb("p2_sinq", [128, 1024], F32, es2)
            masks_sb = sb("p2_masks", [128, NSLOT * 512], BF16, es2)
            rq_sb = sb("p2_rq", [128, 128], BF16, es2)
            banks = [ps(f"ps_b{i}", es2) for i in range(8)]

            for dst, src in [(cosq_sb, cosq), (sinq_sb, sinq),
                             (masks_sb, masks), (rq_sb, rq_in)]:
                tk = P.dma("gpsimd", dst[:], src)
                P.wait_t("tensor", tk)
                P.wait_t("vector", tk)
                P.wait_t("scalar", tk)

            wq3 = wqb[:].rearrange("p (k f) -> p k f", f=384)
            wv3 = wkvb[:].rearrange("p (k f) -> p k f", f=512)
            qn3 = g_qnorm[:].rearrange("p (k f) -> p k f", f=1024)
            cn3 = [c[:] for c in g_cnorm]

            bank_free = [("v", 0)] * 8
            pt_tok = [None, None]
            omm_mark = [0, 0]
            exp_mark = [0, 0]
            last_exp_a = 0
            zrec_v = [0, 0]
            o_war = [None, None]
            qmm_mark = kvmm_mark = 0
            wkv_tok_l = [None]
            xpb_war = 0
            rot = [0]
            kn_ready = [0] * npairs
            v_ready = [0] * npairs

            def emit_qkv_A(p):
                nonlocal qmm_mark, xpb_war
                wait("gpsimd", "p", qmm_mark)
                wq_tok = P.dma("gpsimd", wq3,
                               wq_b[p].rearrange("(k p) f -> p k f", p=128))
                wait("gpsimd", "p", kvmm_mark)
                wkv_tok_l[0] = P.dma("gpsimd", wv3,
                                     wkv_b[p].rearrange("(k p) f -> p k f", p=128))
                P.wait_t("tensor", wq_tok)
                xp_v = 0
                for mi, (lo, hi) in enumerate([(0, 128), (128, 256), (256, 384)]):
                    for half in range(2):
                        bi = rot[0] % 6
                        rot[0] += 1
                        wait("tensor", *bank_free[bi])
                        for kc in range(12):
                            do("tensor", "matmul", banks[bi][0:hi - lo, :],
                               wq3[:, kc, lo:hi],
                               qn3[:, kc, half * 512:(half + 1) * 512],
                               start=(kc == 0), stop=(kc == 11),
                               inc=("p" if kc == 11 else None))
                        pmk = mark("p")
                        hsl = slice(half * 512, (half + 1) * 512)
                        if mi < 2:
                            wait("vector", "p", pmk)
                            do("vector", "tensor_scalar_mul", qn_[mi][:, hsl],
                               banks[bi][:], SCALE, inc="v")
                            bank_free[bi] = ("v", mark("v"))
                        else:
                            wait("vector", "p", pmk)
                            do("vector", "tensor_copy", xpf[:, hsl], banks[bi][:],
                               inc="v")
                            xp_v = mark("v")
                            wait("scalar", "v", xp_v)
                            wait("scalar", "p", xpb_war)
                            do("scalar", "activation", xpb[:, hsl], xpf[:, hsl],
                               AF.Copy, inc="a")
                            bank_free[bi] = ("v", xp_v)
                qmm_mark = mark("p")
                xp_a = mark("a")
                for half in range(2):
                    bi = rot[0] % 6
                    rot[0] += 1
                    hsl = slice(half * 512, (half + 1) * 512)
                    wait("tensor", *bank_free[bi])
                    wait("tensor", "a", xp_a)
                    do("tensor", "matmul", banks[bi][:], rq_sb[:], xpb[:, hsl],
                       start=True, stop=True, inc="p")
                    prq = mark("p")
                    wait("vector", "v", xp_v)
                    do("vector", "tensor_mul", t1f[:, hsl], xpf[:, hsl],
                       cosq_sb[:, hsl], inc="v")
                    wait("vector", "p", prq)
                    do("vector", "tensor_mul", t2f[:, hsl], banks[bi][:],
                       sinq_sb[:, hsl], inc="v")
                    v2 = mark("v")
                    wait("vector", "v", v2)
                    do("vector", "tensor_add", qpe[:, hsl], t1f[:, hsl],
                       t2f[:, hsl], inc="v")
                    bank_free[bi] = ("v", v2)
                xpb_war = mark("p")
                P.wait_t("tensor", wkv_tok_l[0])
                for h01 in range(2):
                    for nt in range(8):
                        bi = rot[0] % 6
                        rot[0] += 1
                        wait("tensor", *bank_free[bi])
                        for kc in range(4):
                            do("tensor", "matmul", banks[bi][:],
                               wv3[:, kc, h01 * 128:(h01 + 1) * 128],
                               cn3[kc][:, nt * 512:(nt + 1) * 512],
                               start=(kc == 0), stop=(kc == 3),
                               inc=("p" if kc == 3 else None))
                        pmk = mark("p")
                        wait("vector", "p", pmk)
                        do("vector", "tensor_copy",
                           kn[h01][:, nt * 512:(nt + 1) * 512], banks[bi][:], inc="v")
                        bank_free[bi] = ("v", mark("v"))
                kn_ready[p] = mark("v")

            def emit_qkv_V(p):
                nonlocal kvmm_mark
                vwar = max(omm_mark)
                for tc in range(32):
                    bi = rot[0] % 6
                    rot[0] += 1
                    wait("tensor", *bank_free[bi])
                    for kc in range(4):
                        do("tensor", "matmul", banks[bi][:, 0:256],
                           cn3[kc][:, tc * 128:(tc + 1) * 128],
                           wv3[:, kc, 256:512],
                           start=(kc == 0), stop=(kc == 3),
                           inc=("p" if kc == 3 else None))
                    pmk = mark("p")
                    wait("vector", "p", pmk)
                    if tc == 0:
                        wait("vector", "p", vwar)
                    do("vector", "tensor_copy", vp[:, tc * 256:(tc + 1) * 256],
                       banks[bi][:, 0:256], inc="v")
                    bank_free[bi] = ("v", mark("v"))
                kvmm_mark = mark("p")
                v_ready[p] = mark("v")

            if True:
                def stg1(st):
                    h01, s = st["h01"], st["s"]
                    nk5, so = st["nk5"], st["so"]
                    for c5 in range(nk5):
                        bi = 2 + 2 * h01 + (c5 % 2)
                        sbk = banks[bi]
                        wait("tensor", *bank_free[bi])
                        do("tensor", "matmul", sbk[:],
                           qn_[h01][:, s * 128:(s + 1) * 128],
                           kn[h01][:, c5 * 512:(c5 + 1) * 512],
                           start=True, stop=False)
                        do("tensor", "matmul", sbk[:],
                           qpe[64 * h01:64 * h01 + 64, s * 128:(s + 1) * 128],
                           g_kpe[64 * h01:64 * h01 + 64, c5 * 512:(c5 + 1) * 512],
                           start=False, stop=True, inc="p")
                        pmk = mark("p")
                        csl = slice(so + c5 * 512, so + (c5 + 1) * 512)
                        if c5 == nk5 - 1:
                            wait("vector", "p", pmk)
                            wait("vector", "a", exp_mark[h01])
                            do("vector", "scalar_tensor_tensor", sco[:, csl],
                               sbk[:], 0.0, masks_sb[:, s * 512:(s + 1) * 512],
                               ALU.bypass, ALU.add, inc="v")
                            em = ("v", mark("v"))
                        else:
                            wait("scalar", "p", pmk)
                            wait("scalar", "a", exp_mark[h01])
                            do("scalar", "activation", sco[:, csl], sbk[:],
                               AF.Copy, inc="a")
                            em = ("a", mark("a"))
                        bank_free[bi] = em
                        st.setdefault("ems", []).append(em)
                    h01_, nk5_, so_ = st["h01"], st["nk5"], st["so"]
                    for em in st["ems"]:
                        wait("vector", em[0], em[1])
                    wait("vector", "a", exp_mark[h01_])
                    do("vector", "reduce_max", nmx[h01_][:],
                       sco[:, so_:so_ + nk5_ * 512], axis=AX.X, negate=True, inc="v")
                    st["vmx"] = mark("v")

                def stg2(st):
                    h01 = st["h01"]
                    nk5, so = st["nk5"], st["so"]
                    wait("scalar", "v", st["vmx"])
                    wait("scalar", "v", zrec_v[h01])
                    P.wait_t("scalar", pt_tok[h01])
                    do("scalar", "activation", pbuf[h01][:, 0:nk5 * 512],
                       sco[:, so:so + nk5 * 512], AF.Exp, bias=nmx[h01][:],
                       accum_out=zs[h01][:], inc="a")
                    exp_mark[h01] = mark("a")
                    wait("vector", "a", exp_mark[h01])
                    do("vector", "reciprocal", zr[h01][:], zs[h01][:], inc="v")
                    zrec_v[h01] = mark("v")
                    wait("scalar", "v", zrec_v[h01])
                    do("scalar", "mul", pbuf[h01][:, 0:nk5 * 512],
                       pbuf[h01][:, 0:nk5 * 512], zr[h01][:], inc="a")
                    st["pn_a"] = mark("a")

                def stg3(st):
                    h01, s, nk1 = st["h01"], st["s"], st["nk1"]
                    wait("sync", "a", st["pn_a"])
                    wait("sync", "p", omm_mark[h01])
                    pt_tok[h01] = P.dma(
                        "sync",
                        ptb[h01][:].rearrange("p (k f) -> p k f", f=128)[:, 0:nk1, :],
                        pbuf[h01][:].rearrange("p (k f) -> p k f", f=128)[:, 0:nk1, :],
                        meth="dma_start_transpose")
                    pt3 = ptb[h01][:].rearrange("p (k f) -> p k f", f=128)
                    P.wait_t("tensor", pt_tok[h01])
                    wait("tensor", "v", v_ready[st["p"]])
                    wait("tensor", *bank_free[6 + h01])
                    for c1 in range(nk1):
                        do("tensor", "matmul", banks[6 + h01][:, 0:128],
                           vp[:, c1 * 256 + h01 * 128:c1 * 256 + h01 * 128 + 128],
                           pt3[:, c1, :],
                           start=(c1 == 0), stop=(c1 == nk1 - 1),
                           inc=("p" if c1 == nk1 - 1 else None))
                    omm_mark[h01] = mark("p")
                    wait("vector", "p", omm_mark[h01])
                    P.wait_t("vector", o_war[h01])
                    do("vector", "tensor_copy", osb[h01][:, s * 128:(s + 1) * 128],
                       banks[6 + h01][:, 0:128], inc="v")
                    bank_free[6 + h01] = ("v", mark("v"))
                    if st["stp"] == NSLOT - 1:
                        wait("gpsimd", "v", bank_free[6 + h01][1])
                        o_war[h01] = P.dma("gpsimd", o_scr[2 * st["p"] + h01],
                                           osb[h01][:])

                steps_all = []
                for p in range(npairs):
                    for stp in range(NSLOT):
                        for h01 in ((0, 1) if p % 2 == 0 else (1, 0)):
                            up = (h01 == 0) == (p % 2 == 0)
                            s = stp if up else NSLOT - 1 - stp
                            nk5 = CAPS512[s]
                            steps_all.append(dict(
                                p=p, h01=h01, s=s, stp=stp, nk5=nk5,
                                nk1=CAPS128[s],
                                so=0 if h01 == 0 else (9 - nk5) * 512))
                T = len(steps_all)
                for g in range(T + 2):
                    if g < T and g % 16 == 0:
                        emit_qkv_A(g // 16)
                        if g == 0:
                            emit_qkv_V(0)
                        wait("tensor", "v", kn_ready[g // 16])
                    b0 = g % 16 == 0
                    b1 = g % 16 == 1
                    if b1 and 2 <= g:
                        stg3(steps_all[g - 2])
                    if b0 and 1 <= g <= T:
                        stg2(steps_all[g - 1])
                    if g < T:
                        stg1(steps_all[g])
                    if (not b0) and 1 <= g <= T:
                        stg2(steps_all[g - 1])
                    if (not b1) and 2 <= g:
                        stg3(steps_all[g - 2])
                    if 2 <= g and (g - 2) % 16 == 15 and (g - 2) // 16 + 1 < npairs:
                        emit_qkv_V((g - 2) // 16 + 1)
            P.barrier()
            es2.close()
            es_mid.close()
            if debug >= 2:
                P.dma("gpsimd", dbg["d_o"], o_scr)

        if debug == 1:
            es_mid.close()
            zt = sb("dbg_zero", [128, D], F32, es_root)
            zq = do("gpsimd", "memset", zt[:], 0.0, inc="q")
            wait("gpsimd", "q", zq)
            for t in range(NSLOT):
                P.dma("gpsimd", out[t * 128:(t + 1) * 128, :], zt[:])
        else:
            es3 = ExitStack()
            hs = [sb(f"p3_hs{i}", [112, 1024], BF16, es3) for i in range(2)]
            es3a = ExitStack()
            at_sb = sb("p3_at", [128, 32 * 224], BF16, es3a)
            gwt_sb = sb("p3_gwt", [128, 32 * 28], BF16, es3a)
            e_sb = sb("p3_esel", [NEXP, 224], F32, es3a)
            xk = [sb(f"p3_xk{i}", [128, 1024], BF16, es3a) for i in range(2)]
            lg_sb = sb("p3_lg", [128, 224], F32, es3a)
            gate_sb = sb("p3_gate", [128, 224], F32, es3a)
            v1s = sb("p3_v1", [128, 1], F32, es3a)
            v2s = sb("p3_v2", [128, 1], F32, es3a)
            is1 = sb("p3_is1", [128, 28], F32, es3a)
            is2 = sb("p3_is2", [128, 28], F32, es3a)
            mskd = sb("p3_mskd", [128, 28], F32, es3a)
            tmpa = sb("p3_tmpa", [128, 28], F32, es3a)
            p1t = sb("p3_p1t", [128, 1], F32, es3a)
            p1o = sb("p3_p1o", [128, 1], F32, es3a)
            p2o = sb("p3_p2o", [128, 1], F32, es3a)
            gate_t = sb("p3_gatet", [NEXP, 1024], F32, es3a)
            gmat = [sb(f"p3_gm{i}", [112, 1024], F32, es3a) for i in range(2)]
            ps_l = ps("ps_l", es3a)
            ps_h = [ps(f"ps_h{i}", es3a) for i in range(4)]
            ps_g = [ps(f"ps_g{i}", es3a) for i in range(2)]

            t_at = P.dma("gpsimd", at_sb[:].rearrange("p (k f) -> p k f", k=32),
                         a_t.rearrange("(k p) f -> p k f", p=128))
            t_gw = P.dma("gpsimd", gwt_sb[:].rearrange("p (k f) -> p k f", k=32),
                         gate_wt.rearrange("(k p) f -> p k f", p=128))
            t_es = P.dma("gpsimd", e_sb[:], esel_in)
            P.wait_t("tensor", t_at)
            P.wait_t("tensor", t_gw)
            at3 = at_sb[:].rearrange("p (k f) -> p k f", f=224)
            gw3 = gwt_sb[:].rearrange("p (k f) -> p k f", f=28)
            xk_used = [0, 0]
            for kc in range(32):
                par = kc % 2
                wait("gpsimd", "p", xk_used[par])
                txk = P.dma("gpsimd", xk[par][:], xnT_scr[kc])
                P.wait_t("tensor", txk)
                for tch in range(8):
                    do("tensor", "matmul", ps_l[:, tch * 28:(tch + 1) * 28],
                       xk[par][:, tch * 128:(tch + 1) * 128], gw3[:, kc, :],
                       start=(kc == 0 and tch == 0), stop=(kc == 31),
                       skip_group_check=True)
                for er in range(2):
                    for half in range(2):
                        do("tensor", "matmul", ps_h[er * 2 + half][0:112, :],
                           at3[:, kc, er * 112:(er + 1) * 112],
                           xk[par][:, half * 512:(half + 1) * 512],
                           start=(kc == 0), stop=(kc == 31),
                           skip_group_check=True,
                           inc=("p" if (er == 1 and half == 1) else None))
                xk_used[par] = mark("p")
            h_done = mark("p")
            wait("vector", "p", h_done)
            do("vector", "tensor_copy", lg_sb[:], ps_l[:, 0:224], inc="v")
            wait("vector", "v", mark("v"))
            gt_marks = []
            for tch in range(8):
                lo = lg_sb[:, tch * 28:(tch + 1) * 28]
                do("vector", "reduce_max", v1s[:], lo, axis=AX.X, inc="v")
                wait("vector", "v", mark("v"))
                do("vector", "tensor_single_scalar", is1[:], lo, v1s[:],
                   ALU.is_ge, inc="v")
                wait("vector", "v", mark("v"))
                do("vector", "scalar_tensor_tensor", mskd[:], is1[:], -1e30,
                   lo, ALU.mult, ALU.add, inc="v")
                wait("vector", "v", mark("v"))
                do("vector", "reduce_max", v2s[:], mskd[:], axis=AX.X, inc="v")
                wait("vector", "v", mark("v"))
                do("vector", "tensor_single_scalar", is2[:], mskd[:], v2s[:],
                   ALU.is_ge, inc="v")
                do("vector", "tensor_sub", p1t[:], v1s[:], v2s[:], inc="v")
                vps = mark("v")
                wait("scalar", "v", vps)
                do("scalar", "activation", p1o[:], p1t[:], AF.Sigmoid, inc="a")
                aps = mark("a")
                wait("vector", "a", aps)
                do("vector", "tensor_scalar", p2o[:], p1o[:], -1.0, 1.0,
                   ALU.mult, op1=ALU.add, inc="v")
                do("vector", "tensor_scalar", tmpa[:], is1[:], p1o[:], None,
                   ALU.mult, inc="v")
                wait("vector", "v", mark("v"))
                do("vector", "tensor_scalar", is2[:], is2[:], p2o[:], None,
                   ALU.mult, inc="v")
                wait("vector", "v", mark("v"))
                do("vector", "tensor_add", gate_sb[:, tch * 28:(tch + 1) * 28],
                   tmpa[:], is2[:], inc="v")
                gv = mark("v")
                wait("tensor", "v", gv)
                do("tensor", "transpose",
                   ps_g[tch // 4][0:NEXP, (tch % 4) * 128:(tch % 4) * 128 + 128],
                   gate_sb[:, tch * 28:(tch + 1) * 28], g_ident[:],
                   inc="p")
                gt_marks.append(mark("p"))
            for gb in range(2):
                wait("vector", "p", gt_marks[gb * 4 + 3])
                do("vector", "tensor_copy", gate_t[:, gb * 512:(gb + 1) * 512],
                   ps_g[gb][0:NEXP, :], inc="v")
            gt_v = mark("v")
            if debug >= 3:
                wait("gpsimd", "v", gt_v)
                P.dma("gpsimd", dbg["d_lg"], lg_sb[:])
                P.dma("gpsimd", dbg["d_gate"], gate_sb[:])
                P.dma("gpsimd", dbg["d_gatet"], gate_t[:])
            P.wait_t("tensor", t_es)
            wait("tensor", "v", gt_v)
            gm_marks = []
            for er in range(2):
                for half in range(2):
                    if er == 1:
                        wait("tensor", "v", gm_marks[half])
                    do("tensor", "matmul", ps_g[half][0:112, :],
                       e_sb[:, er * 112:(er + 1) * 112],
                       gate_t[:, half * 512:(half + 1) * 512],
                       start=True, stop=True, inc="p")
                    gmm = mark("p")
                    wait("vector", "p", gmm)
                    do("vector", "tensor_copy",
                       gmat[er][:, half * 512:(half + 1) * 512],
                       ps_g[half][0:112, :], inc="v")
                    gm_marks.append(mark("v"))
            for er in range(2):
                for half in range(2):
                    wait("vector", "v", gm_marks[er * 2 + half])
                    do("vector", "tensor_mul", hs[er][:, half * 512:(half + 1) * 512],
                       ps_h[er * 2 + half][0:112, :],
                       gmat[er][:, half * 512:(half + 1) * 512], inc="v")
            hs_v = mark("v")
            if debug >= 3:
                wait("gpsimd", "v", hs_v)
                P.dma("gpsimd", dbg["d_hs"][0:112, :], hs[0][:])
                P.dma("gpsimd", dbg["d_hs"][112:224, :], hs[1][:])
            P.barrier()
            es3a.close()

            es3b = ExitStack()
            o_full = sb("p3_of", [128, 32 * 1024], BF16, es3b)
            xn_full = sb("p3_xf", [128, 32 * 1024], BF16, es3b)
            wo_t = [sb(f"p3_wt{i}", [128, 512], BF16, es3b) for i in range(4)]
            bw_t = [sb(f"p3_bw{i}", [128, 512], BF16, es3b) for i in range(4)]
            bt_sb = [sb(f"p3_bt{i}", [112, 4096], BF16, es3b) for i in range(2)]
            xslab = [sb(f"p3_xs{i}", [128, 8 * 512], F32, es3b) for i in range(2)]
            out_st = [sb(f"p3_os{i}", [128, 512], F32, es3b) for i in range(2)]
            ps_acc = [ps(f"ps_acc{i}", es3b) for i in range(8)]

            t_of = P.dma("gpsimd", o_full[:].rearrange("p (k f) -> p k f", k=32),
                         o_scr.rearrange("h p f -> p h f"))
            t_xf = P.dma("gpsimd", xn_full[:].rearrange("p (k f) -> p k f", k=32),
                         xnT_scr.rearrange("h p f -> p h f"))
            tb0 = P.dma("gpsimd", bt_sb[0][:], b_t[0:112, :])
            tb1 = P.dma("gpsimd", bt_sb[1][:], b_t[112:224, :])
            for tk in (t_of, t_xf, tb0, tb1):
                P.wait_t("tensor", tk)
            of3 = o_full[:].rearrange("p (k f) -> p k f", f=1024)
            xf3 = xn_full[:].rearrange("p (k f) -> p k f", f=1024)
            wt_used = [0, 0, 0, 0]
            xs_used = [0, 0]
            acc_free = [("v", 0)] * 8
            os_war = [None, None]
            for dc in range(8):
                pdc = dc % 2
                wait("gpsimd", "v", xs_used[pdc])
                t_xs = P.dma(
                    "gpsimd",
                    xslab[pdc][:].rearrange("p (k f) -> p k f", k=8),
                    x_own[:, dc * 512:(dc + 1) * 512].rearrange("(k p) f -> p k f", p=128))
                P.wait_t("vector", t_xs)
                for kc in range(32):
                    par = kc % 4
                    wait("gpsimd", "p", wt_used[par])
                    t_wo = P.dma("gpsimd", wo_t[par][:],
                                 wo[kc * 128:(kc + 1) * 128, dc * 512:(dc + 1) * 512])
                    t_bw = P.dma("gpsimd", bw_t[par][:],
                                 base_wt[kc * 128:(kc + 1) * 128, dc * 512:(dc + 1) * 512])
                    P.wait_t("tensor", t_wo)
                    P.wait_t("tensor", t_bw)
                    for qc in range(8):
                        if kc == 0:
                            wait("tensor", *acc_free[qc])
                        do("tensor", "matmul", ps_acc[qc][:],
                           of3[:, kc, qc * 128:(qc + 1) * 128], wo_t[par][:],
                           start=(kc == 0), stop=False)
                    for qc in range(8):
                        do("tensor", "matmul", ps_acc[qc][:],
                           xf3[:, kc, qc * 128:(qc + 1) * 128], bw_t[par][:],
                           start=False, stop=False,
                           inc=("p" if qc == 7 else None))
                    wt_used[par] = mark("p")
                qc_done = []
                for qc in range(8):
                    for er in range(2):
                        do("tensor", "matmul", ps_acc[qc][:],
                           hs[er][:, qc * 128:(qc + 1) * 128],
                           bt_sb[er][:, dc * 512:(dc + 1) * 512],
                           start=False, stop=(er == 1),
                           inc=("p" if (er == 1) else None))
                    qc_done.append(mark("p"))
                for qc in range(8):
                    pq = qc % 2
                    wait("vector", "p", qc_done[qc])
                    P.wait_t("vector", os_war[pq])
                    do("vector", "tensor_add", out_st[pq][:], ps_acc[qc][:],
                       xslab[pdc][:, qc * 512:(qc + 1) * 512], inc="v")
                    acc_free[qc] = ("v", mark("v"))
                    wait("gpsimd", "v", acc_free[qc][1])
                    os_war[pq] = P.dma(
                        "gpsimd",
                        out[qc * 128:(qc + 1) * 128, dc * 512:(dc + 1) * 512],
                        out_st[pq][:])
                xs_used[pdc] = mark("v")
            es3b.close()
            es3.close()

        P.wait_all_dma("gpsimd", "gpsimd")
        P.wait_all_dma("gpsimd", "sync")

        block = es_root.enter_context(nc.Block())
        for eng in ENGS:
            getattr(block, eng)(lambda e, q=P.q[eng]: [f(e) for f in q])
    return nc



def _prep_shared(i):
    perm = np.concatenate([np.arange(0, ROPE, 2), np.arange(1, ROPE, 2)])
    wq_b = np.asarray(i["L0_attn_Wq_b"])
    wqb_p = np.empty((NPAIR, QR, 384), np.float32)
    for p in range(NPAIR):
        h0, h1 = 2 * p, 2 * p + 1
        c0 = wq_b[:, h0 * QD:h0 * QD + NOPE]
        c1 = wq_b[:, h1 * QD:h1 * QD + NOPE]
        p0 = wq_b[:, h0 * QD + NOPE:(h0 + 1) * QD][:, perm]
        p1 = wq_b[:, h1 * QD + NOPE:(h1 + 1) * QD][:, perm]
        wqb_p[p] = np.concatenate([c0, c1, p0, p1], axis=1)
    wkv_a = np.array(i["L0_attn_Wkv_a"])
    wkv_a[:, KVR:] = wkv_a[:, KVR:][:, perm]
    wkv_b = np.asarray(i["L0_attn_Wkv_b"])
    wkvb_p = np.empty((NPAIR, KVR, 512), np.float32)
    for p in range(NPAIR):
        h0, h1 = 2 * p, 2 * p + 1
        kn0 = wkv_b[:, h0 * 256:h0 * 256 + NOPE]
        kn1 = wkv_b[:, h1 * 256:h1 * 256 + NOPE]
        v0 = wkv_b[:, h0 * 256 + NOPE:(h0 + 1) * 256]
        v1 = wkv_b[:, h1 * 256 + NOPE:(h1 + 1) * 256]
        wkvb_p[p] = np.concatenate([kn0, kn1, v0, v1], axis=1)
    a_t = np.ascontiguousarray(
        np.asarray(i["L0_ffn_lora_A"]).reshape(NEXP * RANK, D).T)
    b_t = np.ascontiguousarray(
        np.asarray(i["L0_ffn_lora_B"]).transpose(0, 2, 1).reshape(NEXP * RANK, D))
    base_wt = np.ascontiguousarray(np.asarray(i["L0_ffn_base_W"]).T)
    gate_wt = np.ascontiguousarray(np.asarray(i["L0_ffn_gate_W"]).T)
    qlnw = np.ascontiguousarray(
        np.asarray(i["L0_attn_q_a_ln_w"]).reshape(12, 128).T)
    kvlnw = np.ascontiguousarray(
        np.asarray(i["L0_attn_kv_a_ln_w"]).reshape(4, 128).T)

    invf = (1.0 / 10000.0 ** (np.arange(0, ROPE, 2, dtype=np.float64) / ROPE))
    pos = np.arange(S, dtype=np.float64)
    fr = np.outer(invf, pos)
    cosk = np.cos(fr).astype(np.float32)
    sink = np.sin(fr).astype(np.float32)
    cosk = np.concatenate([cosk, cosk], axis=0)
    sink = np.concatenate([sink, sink], axis=0)

    rq = np.zeros((128, 128), np.float32)
    for hb in (0, 64):
        for f in range(32):
            rq[hb + f + 32, hb + f] = -1.0
            rq[hb + f, hb + f + 32] = 1.0
    rk = np.zeros((64, 64), np.float32)
    for f in range(32):
        rk[f + 32, f] = -1.0
        rk[f, f + 32] = 1.0

    ident = np.eye(128, dtype=np.float32)
    esel = np.zeros((NEXP, 224), np.float32)
    for e in range(14):
        esel[e, e * 8:(e + 1) * 8] = LORA_SCALING
    for e in range(14, 28):
        esel[e, 112 + (e - 14) * 8:112 + (e - 13) * 8] = LORA_SCALING

    return dict(
        wq_a=np.ascontiguousarray(np.asarray(i["L0_attn_Wq_a"], np.float32)),
        wq_b=wqb_p, wkv_a=np.ascontiguousarray(wkv_a), wkv_b=wkvb_p,
        wo=np.ascontiguousarray(np.asarray(i["L0_attn_Wo"], np.float32)),
        base_wt=base_wt, gate_wt=gate_wt, a_t=a_t, b_t=b_t,
        qlnw=qlnw, kvlnw=kvlnw, cosk=np.ascontiguousarray(cosk),
        sink=np.ascontiguousarray(sink),
        rq=rq, rk=rk, ident=ident, esel=esel,
    )


def _prep_core(x, shared, j, b):
    tiles = tiles_for(j)
    xb = np.asarray(x[b])
    x_own = np.concatenate([xb[t * 128:(t + 1) * 128] for t in tiles], 0)
    invf = (1.0 / 10000.0 ** (np.arange(0, ROPE, 2, dtype=np.float64) / ROPE))
    posq = np.concatenate(
        [np.arange(t * 128, (t + 1) * 128, dtype=np.float64) for t in tiles])
    fr = np.outer(invf, posq)
    c64 = np.concatenate([np.cos(fr), np.cos(fr)], 0)
    s64 = np.concatenate([np.sin(fr), np.sin(fr)], 0)
    cosq = (np.concatenate([c64, c64], 0) * SCALE).astype(np.float32)
    sinq = (np.concatenate([s64, s64], 0) * SCALE).astype(np.float32)
    masks = np.zeros((128, NSLOT * 512), np.float32)
    for s in range(NSLOT):
        t = tiles[s]
        k0 = CAPS128[s] * 128 - 512
        kk = np.arange(k0, k0 + 512)
        qq = t * 128 + np.arange(128)
        masks[:, s * 512:(s + 1) * 512] = np.where(
            kk[None, :] <= qq[:, None], 0.0, -1e30)
    d = dict(shared)
    d.update(x_full=np.ascontiguousarray(xb), x_own=np.ascontiguousarray(x_own),
             cosq=cosq, sinq=sinq, masks=masks)
    return d


_nc_cache = {}


def _get_nc(debug=0):
    if debug not in _nc_cache:
        _nc_cache[debug] = build_nc(debug)
    return _nc_cache[debug]


def kernel(**inputs):
    shared = _prep_shared(inputs)
    x = np.asarray(inputs["L0_attn_hidden_states"], np.float32)
    in_maps = []
    for c in range(8):
        in_maps.append(_prep_core(x, shared, c % 4, c // 4))
    nc = _get_nc(0)
    res = run_bass_kernel_spmd(nc, in_maps, core_ids=list(range(8)))
    out = np.empty((B, S, D), np.float32)
    for c in range(8):
        tiles = tiles_for(c % 4)
        r = res.results[c]["out"]
        for s, t in enumerate(tiles):
            out[c // 4, t * 128:(t + 1) * 128, :] = r[s * 128:(s + 1) * 128, :]
    return out
